# version 25 (final)
# baseline (speedup 1.0000x reference)
import numpy as np
import ml_dtypes
from contextlib import ExitStack

import concourse.bass as bass
import concourse.mybir as mybir
from concourse.bass_utils import run_bass_kernel_spmd

F32 = mybir.dt.float32
BF16 = mybir.dt.bfloat16
AF = mybir.ActivationFunctionType
ALU = mybir.AluOpType
AXX = mybir.AxisListType.X

B, S, DM = 4, 2048, 2048
NH, HD, NCH = 16, 128, 16
T = 1024
NCORES = 8
INV_SQRT_HD = float(1.0 / np.sqrt(128.0))
EXP_SHIFT = 32.0

BF16NP = ml_dtypes.bfloat16



def _own_blocks(rank):
    return [2 * j + rank for j in range(8)]


def _own_positions(rank):
    return np.concatenate([np.arange(128 * g, 128 * (g + 1)) for g in _own_blocks(rank)])


def _pack128(w2d):
    return np.ascontiguousarray(w2d.reshape(16, 128, 16, 128).transpose(2, 1, 0, 3).reshape(16, 128, 2048))


def _pack256(w2d):
    return np.ascontiguousarray(w2d.reshape(16, 128, 8, 256).transpose(2, 1, 0, 3).reshape(8, 128, 4096))


def _host_prep(inputs):
    shared = {}
    tr = lambda a: np.ascontiguousarray(np.asarray(a).T)
    wqkv = np.asarray(inputs["L0_attn_w_qkv"])
    shared["wqp0"] = _pack128(wqkv[:, 0:2048])
    shared["wkp0"] = _pack128(wqkv[:, 2048:4096])
    shared["wvp0"] = _pack256(wqkv[:, 4096:6144])
    shared["wop0"] = _pack256(np.asarray(inputs["L0_attn_w_out"]))
    shared["gq"] = np.asarray(inputs["L0_attn_q_gamma"]).reshape(128, 1).copy()
    shared["gk"] = np.asarray(inputs["L0_attn_k_gamma"]).reshape(128, 1).copy()
    wgu = tr(inputs["L0_ffn_w_gate_up"])
    shared["wgug"] = np.ascontiguousarray(
        wgu[:, 0:8192].reshape(16, 128, 32, 2, 128).transpose(2, 1, 3, 0, 4).reshape(32, 128, 4096))
    shared["wguu"] = np.ascontiguousarray(
        wgu[:, 8192:16384].reshape(16, 128, 32, 2, 128).transpose(2, 1, 3, 0, 4).reshape(32, 128, 4096))
    wdn = tr(inputs["L0_ffn_w_down"])
    shared["wdnp"] = np.ascontiguousarray(
        wdn.reshape(8, 8, 128, 4, 512).transpose(0, 3, 2, 1, 4).reshape(32, 128, 4096))
    shared["wqp1"] = _pack128(tr(inputs["L1_attn_Wq"]))
    shared["wkp1"] = _pack128(tr(inputs["L1_attn_Wk"]))
    shared["wvp1"] = _pack256(tr(inputs["L1_attn_Wv"]))
    shared["wop1"] = _pack256(tr(inputs["L1_attn_Wo"]))
    shared["wr1p"] = np.ascontiguousarray(
        tr(inputs["L1_ffn_router_w"]).reshape(16, 128, 8).transpose(1, 0, 2).reshape(128, 128))
    wsg, wsu = tr(inputs["L1_ffn_shared_gate"]), tr(inputs["L1_ffn_shared_up"])
    for nm, w_ in (("wsgp", wsg), ("wsup", wsu)):
        a = w_.reshape(16, 128, 1408)
        shared[nm + "0"] = np.ascontiguousarray(a[:, :, 0:768].transpose(1, 0, 2).reshape(128, 12288))
        shared[nm + "1"] = np.ascontiguousarray(a[:, :, 768:1408].transpose(1, 0, 2).reshape(128, 10240))
    shared["wsdp"] = np.ascontiguousarray(
        tr(inputs["L1_ffn_shared_down"]).reshape(11, 128, 2, 1024).transpose(2, 1, 0, 3).reshape(2, 128, 11264))
    wge = np.asarray(inputs["L1_ffn_w_gate"]).transpose(0, 2, 1)
    wue = np.asarray(inputs["L1_ffn_w_up"]).transpose(0, 2, 1)
    for nm, w_ in (("wgep", wge), ("wuep", wue)):
        a = w_.reshape(8, 16, 128, 1408)
        shared[nm + "0"] = np.ascontiguousarray(a[:, :, :, 0:768].transpose(0, 2, 1, 3).reshape(8, 128, 12288))
        shared[nm + "1"] = np.ascontiguousarray(a[:, :, :, 768:1408].transpose(0, 2, 1, 3).reshape(8, 128, 10240))
    wde = np.asarray(inputs["L1_ffn_w_down"]).transpose(0, 2, 1)
    shared["wdep"] = np.ascontiguousarray(
        wde.reshape(8, 11, 128, 2, 1024).transpose(0, 3, 2, 1, 4).reshape(16, 128, 11264))

    ident = np.eye(128, dtype=BF16NP)
    tri = (np.arange(128)[None, :] >= np.arange(128)[:, None]).astype(BF16NP)
    onesc_bf = np.ones((128, 1), dtype=BF16NP)
    onesr_bf = np.ones((1, 128), dtype=BF16NP)
    onesr_f32 = np.ones((1, 128), dtype=np.float32)
    zeros = np.zeros((128, 128), dtype=BF16NP)
    ones128 = np.ones((128, 128), dtype=BF16NP)

    x = np.asarray(inputs["L0_attn_x"])
    percore = []
    for c in range(NCORES):
        b, rank = divmod(c, 2)
        pos = _own_positions(rank)
        d = {}
        d["xin"] = np.ascontiguousarray(x[b].T[:, pos])
        inv_freq = (10000.0 ** (-np.arange(0, 128, 2, dtype=np.float32) / 128.0))
        ang = pos.astype(np.float32)[None, :] * inv_freq[:, None]
        d["cost"] = np.cos(ang).astype(np.float32)
        d["sint"] = np.sin(ang).astype(np.float32)
        d["m0"] = tri if rank == 0 else ones128
        d["m1"] = zeros if rank == 0 else tri
        d["m01"] = np.concatenate([d["m0"], d["m1"]], axis=1)
        d["ident"] = ident
        d["onesc_bf"] = onesc_bf
        d["onesr_bf"] = onesr_bf
        d["onesr_f32"] = onesr_f32
        d["bcols"] = np.repeat(np.array([[1e-5, 1e-6, -EXP_SHIFT, 0.0]], dtype=np.float32), 128, axis=0)
        percore.append(d)
    return shared, percore


def _assemble_output(results):
    out = np.empty((B, S, DM), dtype=np.float32)
    for c in range(NCORES):
        b, rank = divmod(c, 2)
        pos = _own_positions(rank)
        out[b, pos, :] = results[c]["xout"].T
    return out



ENGINES = ("pe", "act", "dve", "pool", "sp")
_BLOCK_ATTR = {"pe": "tensor", "act": "scalar", "dve": "vector", "pool": "gpsimd", "sp": "sync"}


class Prog:
    def __init__(self, nc):
        self.nc = nc
        self.ops = {e: [] for e in ENGINES}
        self.lw = {}
        self.rd = {}
        self.slots = {}
        self.slot_inc = {}
        self.last_e = {}
        self.n_array = 0
        self.marks = []

    def _dep_events(self, eng, reads, writes):
        deps = []
        for t in reads:
            ev = self.lw.get(t)
            if ev is not None:
                deps.append(ev)
        for t in writes:
            ev = self.lw.get(t)
            if ev is not None:
                deps.append(ev)
            deps.extend(self.rd.get(t, ()))
        ded = {}
        for ev in deps:
            if ev[0] == "e" and ev[1] == eng and eng in ("pe", "sp"):
                continue
            key = (ev[0], ev[1])
            if key not in ded or ded[key][2] < ev[2]:
                ded[key] = ev
        return list(ded.values())

    def _record(self, myev, reads, writes):
        for t in reads:
            self.rd.setdefault(t, []).append(myev)
        for t in writes:
            self.lw[t] = myev
            self.rd[t] = []

    def op(self, eng, fn, reads=(), writes=()):
        waits = self._dep_events(eng, reads, writes)
        idx = len(self.ops[eng])
        myev = ("e", eng, idx)
        self.ops[eng].append(dict(fn=fn, waits=waits, marked=False, slot=None, inc=0))
        self.last_e[eng] = idx
        self._record(myev, reads, writes)

    def dma(self, eng, slot, fn, reads=(), writes=(), inc=16):
        waits = self._dep_events(eng, reads, writes)
        cnt = self.slots.get(slot, 0) + 1
        self.slots[slot] = cnt
        self.slot_inc[slot] = inc
        myev = ("d", slot, cnt)
        self.ops[eng].append(dict(fn=fn, waits=waits, marked=False, slot=slot, inc=inc))
        self._record(myev, reads, writes)

    def barrier(self, exclude_slots=()):
        evs = []
        for e in ENGINES:
            if e in self.last_e:
                evs.append(("e", e, self.last_e[e]))
        for s, c in self.slots.items():
            if s in exclude_slots:
                continue
            evs.append(("d", s, c))
        for e in ENGINES:
            ded = [ev for ev in evs if not (ev[0] == "e" and ev[1] == e)]
            idx = len(self.ops[e])
            self.ops[e].append(dict(fn=lambda eng: eng.nop(nofuse=True), waits=ded, marked=False, slot=None, inc=0))
        keep = {t: ev for t, ev in self.lw.items()
                if ev[0] == "d" and ev[1] in exclude_slots}
        self.lw = keep
        self.rd = {t: [] for t in keep}

    def finalize(self, block, ctx):
        nc = self.nc
        sems = {}
        for e in ENGINES:
            sems[e] = ctx.enter_context(nc.semaphore(f"sem_{e}"))
        for s in self.slots:
            sems[s] = ctx.enter_context(nc.semaphore(f"sem_{s}"))
        for e in ENGINES:
            for op in self.ops[e]:
                for ev in op["waits"]:
                    if ev[0] == "e":
                        self.ops[ev[1]][ev[2]]["marked"] = True
        rank = {}
        for e in ENGINES:
            c, r = 0, []
            for op in self.ops[e]:
                if op["marked"]:
                    c += 1
                r.append(c)
            rank[e] = r

        def mk_runner(ename):
            ops = self.ops[ename]

            def run(eng):
                waited = {}
                for op in ops:
                    for ev in op["waits"]:
                        if ev[0] == "e":
                            key, val = ev[1], rank[ev[1]][ev[2]]
                        else:
                            key, val = ev[1], self.slot_inc[ev[1]] * ev[2]
                        if waited.get(key, -1) >= val:
                            continue
                        waited[key] = val
                        eng.wait_ge(sems[key], val)
                    inst = op["fn"](eng)
                    if op["slot"] is not None:
                        inst.then_inc(sems[op["slot"]], op["inc"]) if op["inc"] != 1 else inst.then_inc(sems[op["slot"]])
                    elif op["marked"]:
                        inst.then_inc(sems[ename], 1)
            return run

        for e in ENGINES:
            getattr(block, _BLOCK_ATTR[e])(mk_runner(e))



def MM(P, out_ap, lhsT_ap, rhs_ap, start, stop, r, w):
    P.n_array += 1
    if hasattr(P, "mm_labels"):
        P.mm_labels.append(("+".join(r[:2]), str(rhs_ap.tensor.dtype), rhs_ap.shape[-1] if hasattr(rhs_ap, "shape") else 0))
    P.op("pe", lambda e, o=out_ap, l=lhsT_ap, rr=rhs_ap, s1=start, s2=stop:
         e.matmul(o, l, rr, start=s1, stop=s2), reads=r, writes=w)


def ACT(P, out_ap, in_ap, func, r, w, bias=0.0, scale=1.0):
    P.op("act", lambda e, o=out_ap, i=in_ap, f=func, b=bias, s=scale:
         e.activation(o, i, f, bias=b, scale=s), reads=r, writes=w)


def TT(P, eng, out_ap, a_ap, b_ap, alu, r, w):
    P.op(eng, lambda e, o=out_ap, a=a_ap, b=b_ap, op=alu:
         e.tensor_tensor(o, a, b, op), reads=r, writes=w)


def TS(P, eng, out_ap, in_ap, scalar, alu, r, w):
    P.op(eng, lambda e, o=out_ap, i=in_ap, s=scalar, op=alu:
         e.tensor_single_scalar(o, i, s, op), reads=r, writes=w)


def STT(P, eng, out_ap, in0_ap, scalar, in1_ap, op0, op1, r, w):
    P.op(eng, lambda e, o=out_ap, i0=in0_ap, s=scalar, i1=in1_ap, a=op0, b=op1:
         e.scalar_tensor_tensor(o, i0, s, i1, a, b), reads=r, writes=w)


def RED(P, out_ap, in_ap, alu, r, w, negate=False):
    P.op("dve", lambda e, o=out_ap, i=in_ap, op=alu, n=negate:
         e.tensor_reduce(o, i, AXX, op, negate=n), reads=r, writes=w)


def RECIP(P, out_ap, in_ap, r, w):
    P.op("dve", lambda e, o=out_ap, i=in_ap: e.reciprocal(o, i), reads=r, writes=w)


def RECIPB(P, out_ap, in_ap, r, w):
    nc = P.nc

    def fn(e, o=out_ap, i=in_ap):
        with nc.allow_low_precision(reason="bf16 broadcast row (within tolerance model)"):
            return e.reciprocal(o, i)
    P.op("dve", fn, reads=r, writes=w)


def DMA(P, eng, slot, out_ap, in_ap, r, w):
    P.dma(eng, slot, lambda e, o=out_ap, i=in_ap: e.dma_start(out=o, in_=i), reads=r, writes=w)



class Ctx:
    def __init__(self):
        self._uid = 0

    def uniq(self, name):
        self._uid += 1
        return f"{name}_{self._uid}"


def _chunk3(ap, p=128):
    return ap.rearrange("(c p) n -> p c n", p=p)


def xft(c, h):
    return f"xf{c}_{h}"


def xbt(c, h):
    return f"xb{c}_{h}"


def emit_ln(P, C, tc, write_xb=True):
    xf, xb = C.xf, C.xb
    t0 = 512 * tc
    for c in range(NCH):
        src = xf[:, 1024 * c + t0: 1024 * c + t0 + 512]
        dst = xb[:, 1024 * c + t0: 1024 * c + t0 + 512]
        ACT(P, dst, src, AF.Copy, r=[xft(c, tc)], w=[xbt(c, tc)])
        ACT(P, C.sq[:, 0:512], dst, AF.Square, r=[xbt(c, tc)], w=["sq"])
        MM(P, C.pd0[0:1, :], C.onesc, dst, start=(c == 0), stop=(c == NCH - 1),
           r=[xbt(c, tc), "onesc"], w=["pd0"])
        MM(P, C.pd1[0:1, :], C.onesc, C.sq[:, 0:512], start=(c == 0), stop=(c == NCH - 1),
           r=["sq", "onesc"], w=["pd1"])
    rA, rB, rC, rD = C.rowA[0:1, 0:512], C.rowB[0:1, 0:512], C.rowC[0:1, 0:512], C.rowD[0:1, 0:512]
    TS(P, "dve", rA, C.pd0[0:1, :], 1.0 / 2048.0, ALU.mult, r=["pd0"], w=["rowA"])
    ACT(P, rB, rA, AF.Square, r=["rowA"], w=["rowB"])
    STT(P, "dve", rC, C.pd1[0:1, :], 1.0 / 2048.0, rB, ALU.mult, ALU.subtract, r=["pd1", "rowB"], w=["rowC"])
    ACT(P, rB, rC, AF.Ln, r=["rowC", "bcols"], w=["rowB"], bias=C.b_ln_eps)
    ACT(P, rB, rB, AF.Exp, r=["rowB"], w=["rowB"], scale=-0.5)
    TT(P, "dve", rD, rA, rB, ALU.mult, r=["rowA", "rowB"], w=["rowD"])
    MM(P, C.ps[4][:, :], C.onesr32, rB, start=True, stop=True, r=["rowB", "onesr32"], w=["ps4"])
    MM(P, C.ps[5][:, :], C.onesr32, rD, start=True, stop=True, r=["rowD", "onesr32"], w=["ps5"])
    for c in range(NCH):
        xc = xf[:, 1024 * c + t0: 1024 * c + t0 + 512]
        TT(P, "dve", xc, xc, C.ps[4][:, :], ALU.mult, r=[xft(c, tc), "ps4"], w=[xft(c, tc)])
        TT(P, "dve", xc, xc, C.ps[5][:, :], ALU.subtract, r=[xft(c, tc), "ps5"], w=[xft(c, tc)])
        if write_xb:
            dst = xb[:, 1024 * c + t0: 1024 * c + t0 + 512]
            ACT(P, dst, xc, AF.Copy, r=[xft(c, tc)], w=[xbt(c, tc)])


def emit_ffn0(P, C):
    nc = C.nc
    with ExitStack() as sc:
        sb = lambda name, shape, dt: sc.enter_context(nc.sbuf_tensor(C.uniq(name), shape, dt))
        hblk = [sb(f"ffn_h{i}", [128, 8 * 512], BF16) for i in range(2)]
        acc = sb("ffn_acc", [128, NCH * 512], F32)
        gsl = [sb(f"ffn_g{i}", [128, NCH * 256], BF16) for i in range(2)]
        usl = [sb(f"ffn_u{i}", [128, NCH * 256], BF16) for i in range(2)]
        dsl = [sb(f"ffn_d{i}", [128, 8 * 512], BF16) for i in range(2)]
        sg = sb("ffn_sg", [128, 512], BF16)

        def load_gu(pair, buf, split=True):
            for sub in (range(2) if split else (None,)):
                sl_ = slice(0, 4096) if sub is None else slice(2048 * sub, 2048 * sub + 2048)
                gt = [f"gsl{buf}s0", f"gsl{buf}s1"] if sub is None else [f"gsl{buf}s{sub}"]
                ut = [f"usl{buf}s0", f"usl{buf}s1"] if sub is None else [f"usl{buf}s{sub}"]
                sfx = "" if not sub else "b"
                DMA(P, "pool", f"wg{buf}{sfx}", gsl[buf][:, sl_], C.wgug[pair][:, sl_], r=gt, w=gt)
                DMA(P, "pool", f"wu{buf}{sfx}", usl[buf][:, sl_], C.wguu[pair][:, sl_], r=ut, w=ut)

        def load_dn(ib, ocq, buf):
            for hf in range(2):
                sl_ = slice(2048 * hf, 2048 * hf + 2048)
                DMA(P, "pool", f"wd{buf}" + ("" if hf == 0 else "b"),
                    dsl[buf][:, sl_], C.wdnp[4 * ib + ocq][:, sl_],
                    r=[f"dsl{buf}"], w=[f"dsl{buf}"])

        for tc in range(2):
            t0 = 512 * tc

            def gate_pair(ib, pair):
                hb = ib % 2
                if True:
                    buf = pair % 2
                    if pair + 1 < 32:
                        load_gu(pair + 1, (pair + 1) % 2)
                    elif tc == 0:
                        load_gu(0, 0)
                    for sub in range(2):
                        oc = 2 * pair + sub
                        lo = oc - 8 * ib
                        psg, psu = C.ps[2 * sub], C.ps[2 * sub + 1]
                        for c in range(NCH):
                            lg = gsl[buf][:, 2048 * sub + 128 * c: 2048 * sub + 128 * c + 128]
                            lu = usl[buf][:, 2048 * sub + 128 * c: 2048 * sub + 128 * c + 128]
                            rhs = C.xb[:, 1024 * c + t0: 1024 * c + t0 + 512]
                            MM(P, psg[:, :], lg, rhs, start=(c == 0), stop=(c == NCH - 1),
                               r=[f"gsl{buf}s{sub}", xbt(c, tc)], w=[f"ps{2*sub}"])
                            MM(P, psu[:, :], lu, rhs, start=(c == 0), stop=(c == NCH - 1),
                               r=[f"usl{buf}s{sub}", xbt(c, tc)], w=[f"ps{2*sub+1}"])
                        hc = hblk[hb][:, 512 * lo: 512 * lo + 512]
                        ACT(P, sg[:, :], psg[:, :], AF.Silu, r=[f"ps{2*sub}"], w=["ffnsg"])
                        TT(P, "dve", hc, psu[:, :], sg[:, :], ALU.mult,
                           r=[f"ps{2*sub+1}", "ffnsg"], w=[f"h{hb}_{lo}"])

            def down_quarter(ib, ocq):
                hb = ib % 2
                if True:
                    buf = (ib + ocq) % 2
                    if ocq + 1 < 4:
                        load_dn(ib, ocq + 1, (ib + ocq + 1) % 2)
                    for hq in range(4):
                        hc = 4 * ocq + hq
                        pso = C.ps[4 + (hc % 2)]
                        for ic in range(8):
                            MM(P, pso[:, :], dsl[buf][:, 512 * ic + 128 * hq: 512 * ic + 128 * hq + 128],
                               hblk[hb][:, 512 * ic: 512 * ic + 512], start=(ic == 0), stop=(ic == 7),
                               r=[f"dsl{buf}", f"h{hb}_{ic}"], w=[f"ps{4 + (hc % 2)}"])
                        ac = acc[:, 512 * hc: 512 * hc + 512]
                        if ib == 0:
                            ACT(P, ac, pso[:, :], AF.Copy, r=[f"ps{4 + (hc % 2)}"], w=[f"acc{hc}"])
                        else:
                            TT(P, "dve", ac, pso[:, :], ac, ALU.add,
                               r=[f"ps{4 + (hc % 2)}", f"acc{hc}"], w=[f"acc{hc}"])

            if tc == 0:
                load_gu(0, 0)
            for pair in range(4):
                gate_pair(0, pair)
            for ib in range(1, 8):
                load_dn(ib - 1, 0, (ib - 1) % 2)
                for pair in range(4):
                    gate_pair(ib, 4 * ib + pair)
                    down_quarter(ib - 1, pair)
            load_dn(7, 0, 1)
            for pair in range(4):
                down_quarter(7, pair)
            for c in range(NCH):
                xc = C.xf[:, 1024 * c + t0: 1024 * c + t0 + 512]
                TT(P, "dve", xc, acc[:, 512 * c: 512 * c + 512], xc, ALU.add,
                   r=[f"acc{c}", xft(c, tc)], w=[xft(c, tc)])
            emit_ln(P, C, tc)
        P.barrier()


def emit_attn(P, C, layer):
    nc = C.nc
    L0 = layer == 0

    outer = ExitStack()
    qstage_h = outer.enter_context(nc.sbuf_tensor(C.uniq("at_qstage"), [128, NH * 1024], BF16))
    qstage = qstage_h[:]
    with ExitStack() as sc:
        sb = lambda name, shape, dt: sc.enter_context(nc.sbuf_tensor(C.uniq(name), shape, dt))
        rtab = sb("at_rtab", [128, 4 * 1024], BF16)

        tbl_sc = ExitStack()
        tsb = lambda name, shape, dt: tbl_sc.enter_context(nc.sbuf_tensor(C.uniq(name), shape, dt))
        cosf = tsb("at_cosf", [64, 1024], F32)
        sinf = tsb("at_sinf", [64, 1024], F32)
        scr64 = tsb("at_scr64", [64, 1024], BF16)
        gcol = tsb("at_gcol", [64, 3], F32)
        DMA(P, "sp", "c_cos", cosf[:], C.cost, r=["cos"], w=["cos"])
        DMA(P, "sp", "c_sin", sinf[:], C.sint, r=["sin"], w=["sin"])
        rt = rtab[:]

        def build_tables(ci, si, gam_sb, gtag):
            Ct = rt[:, 1024 * ci: 1024 * ci + 1024]
            St = rt[:, 1024 * si: 1024 * si + 1024]
            if gam_sb is None:
                P.op("dve", lambda e: e.tensor_copy(Ct[0:64, :], cosf[:]), reads=["cos"], writes=[f"rt{ci}"])
                P.op("dve", lambda e: e.tensor_copy(scr64[:], cosf[:]), reads=["cos"], writes=["scr64"])
                P.op("dve", lambda e: e.tensor_copy(Ct[64:128, :], scr64[:]), reads=["scr64"], writes=[f"rt{ci}"])
                TS(P, "dve", St[0:64, :], sinf[:], -1.0, ALU.mult, r=["sin"], w=[f"rt{si}"])
                P.op("dve", lambda e: e.tensor_copy(scr64[:], sinf[:]), reads=["sin"], writes=["scr64"])
                P.op("dve", lambda e: e.tensor_copy(St[64:128, :], scr64[:]), reads=["scr64"], writes=[f"rt{si}"])
                return
            gt0 = gam_sb[0:64, :]
            gb0 = gcol[:, 0:1]
            gb0n = gcol[:, 1:2]
            P.op("dve", lambda e: e.tensor_copy(gb0, gam_sb[64:128, :]), reads=[gtag], writes=["gcol"])
            TS(P, "dve", gb0n, gb0, -1.0, ALU.mult, r=["gcol"], w=["gcol"])
            TS(P, "dve", Ct[0:64, :], cosf[:], gt0, ALU.mult, r=["cos", gtag], w=[f"rt{ci}"])
            TS(P, "dve", scr64[:], cosf[:], gb0, ALU.mult, r=["cos", "gcol"], w=["scr64"])
            P.op("dve", lambda e: e.tensor_copy(Ct[64:128, :], scr64[:]), reads=["scr64"], writes=[f"rt{ci}"])
            TS(P, "dve", St[0:64, :], sinf[:], gb0n, ALU.mult, r=["sin", "gcol"], w=[f"rt{si}"])
            TS(P, "dve", scr64[:], sinf[:], gt0, ALU.mult, r=["sin", gtag], w=["scr64"])
            P.op("dve", lambda e: e.tensor_copy(St[64:128, :], scr64[:]), reads=["scr64"], writes=[f"rt{si}"])

        if L0:
            build_tables(0, 1, C.gq_sb, "gq")
            build_tables(2, 3, C.gk_sb, "gk")
            qtabs, ktabs = (0, 1), (2, 3)
        else:
            build_tables(0, 1, None, None)
            qtabs = ktabs = (0, 1)
        tbl_sc.close()
        P.barrier(exclude_slots=("ccv", "cck0", "cck1", "cck2", "cck3"))

        qsl = [sb(f"at_q{i}", [128, NCH * 128], BF16) for i in range(2)]
        ksl = [sb(f"at_k{i}", [128, NCH * 128], BF16) for i in range(2)]
        vsl = [sb(f"at_v{i}", [128, NCH * 256], BF16) for i in range(2)]
        khro = [sb("at_khro", [128, 1024], BF16)] * 2
        rtmp = sb("at_rtmp", [128, 1024], BF16)
        if L0:
            sqs = {"q": [sb(f"at_sqq{i}", [128, 1024], BF16) for i in range(2)],
                   "k": [sb(f"at_sqk{i}", [128, 1024], BF16) for i in range(2)]}
        vtmp = [sb(f"at_vtmp{i}", [128, 256], BF16) for i in range(2)]
        rawq3 = [sb(f"at_rq{i}", [128, 1024], BF16) for i in range(3)] if L0 else None
        rawk3 = [sb(f"at_rk{i}", [128, 1024], BF16) for i in range(3)] if L0 else None
        qhb3 = [sb(f"at_qhb{i}", [128, 1024], BF16) for i in range(3)] if not L0 else None
        khb3 = [sb(f"at_khb{i}", [128, 1024], BF16) for i in range(3)] if not L0 else None

        def rope(src_ap, src_tag, dst_ap, dst_tag, tabs):
            ci, si = tabs
            Ct = rt[:, 1024 * ci: 1024 * ci + 1024]
            St = rt[:, 1024 * si: 1024 * si + 1024]
            ACT(P, rtmp[0:64, :], src_ap[64:128, :], AF.Copy, r=[src_tag], w=["rtmp"])
            ACT(P, rtmp[64:128, :], src_ap[0:64, :], AF.Copy, r=[src_tag], w=["rtmp"])
            TT(P, "dve", rtmp[:], rtmp[:], St, ALU.mult, r=["rtmp", f"rt{si}"], w=["rtmp"])
            TT(P, "dve", dst_ap, src_ap, Ct, ALU.mult, r=[src_tag, f"rt{ci}"], w=[dst_tag])
            TT(P, "dve", dst_ap, dst_ap, rtmp[:], ALU.add, r=[dst_tag, "rtmp"], w=[dst_tag])

        wqp = C.wqp0 if L0 else C.wqp1
        wkp = C.wkp0 if L0 else C.wkp1
        wvp = C.wvp0 if L0 else C.wvp1

        def load_head(h, hb):
            DMA(P, "pool", f"aq{hb}", qsl[hb][:], wqp[h], r=[f"aq{hb}"], w=[f"aq{hb}"])
            DMA(P, "pool", f"ak{hb}", ksl[hb][:], wkp[h], r=[f"ak{hb}"], w=[f"ak{hb}"])

        cctag = [0]

        def stage(out_ap, in_ap, rtags, acc, sl):
            t = f"ccin{cctag[0]}"
            cctag[0] += 1
            DMA(P, "sp", sl, out_ap, in_ap, r=rtags + [t], w=[t])
            acc.append(t)

        groups = [[2 * i, 2 * i + 1] for i in range(4)]

        def collective(slot, in_ap, out_ap, rtags, outtag):
            P.dma("pool", slot,
                  lambda e, i=in_ap, o=out_ap: e.collective_compute(
                      "AllGather", ALU.bypass, replica_groups=groups, ins=[i.opt()], outs=[o.opt()]),
                  reads=rtags, writes=[outtag], inc=1)

        ccv_tags = []
        cck_tags = [[] for _ in range(4)]

        def load_v(s):
            b = s % 2
            for hf in range(2):
                DMA(P, "pool", f"av{b}" + ("" if hf == 0 else "b"),
                    vsl[b][:, 2048 * hf: 2048 * hf + 2048],
                    wvp[s][:, 2048 * hf: 2048 * hf + 2048],
                    r=[f"av{b}h{hf}"], w=[f"av{b}h{hf}"])

        load_head(0, 0)
        load_head(1, 1)
        load_v(0)
        for s in range(8):
            if s + 1 < 8:
                load_v(s + 1)
            sb_ = s % 2
            for tc in range(8):
                psv = C.ps[4 + tc % 2]
                pst = f"ps{4 + tc % 2}"
                for c in range(NCH):
                    MM(P, psv[:, 0:256], C.xb[:, 1024 * c + 128 * tc: 1024 * c + 128 * tc + 128],
                       vsl[sb_][:, 256 * c: 256 * c + 256], start=(c == 0), stop=(c == NCH - 1),
                       r=[f"av{sb_}h{c // 8}", xbt(c, tc // 4)], w=[pst])
                k_ = 8 * s + tc
                vb = k_ % 2
                ACT(P, vtmp[vb][:, 0:256], psv[:, 0:256], AF.Copy, r=[pst], w=[f"vtmp{vb}"])
                hi, d0 = s // 4, 256 * (s % 4)
                stage(C.ccv_in[256 * tc + hi: 256 * (tc + 1): 2, d0:d0 + 256],
                      vtmp[vb][:, 0:256], [f"vtmp{vb}"], ccv_tags, f"vstg{k_ % 4}")
        collective("ccv", C.ccv_in, C.ccv_out, ccv_tags, "ccoV")

        P.marks.append((f"attnA{layer}-Vdone", P.n_array))
        rawq = rawq3
        rawk = rawk3
        qhb_ = qhb3
        khb_ = khb3

        def s1(h):
            hb = h % 3
            wb = h % 2
            for kind in ("q", "k"):
                sl = qsl[wb] if kind == "q" else ksl[wb]
                stag = f"a{kind}{wb}"
                for qc in range(2):
                    psp = C.ps[qc if kind == "q" else 2 + qc]
                    pst = f"ps{qc if kind == 'q' else 2 + qc}"
                    for c in range(NCH):
                        MM(P, psp[:, :], sl[:, 128 * c: 128 * c + 128],
                           C.xb[:, 1024 * c + 512 * qc: 1024 * c + 512 * qc + 512],
                           start=(c == 0), stop=(c == NCH - 1), r=[stag, xbt(c, qc)], w=[pst])
                    if L0:
                        dst = rawq[hb] if kind == "q" else rawk[hb]
                        dtag = f"raw{kind}{hb}"
                        ACT(P, dst[:, 512 * qc: 512 * qc + 512], psp[:, :], AF.Copy, r=[pst], w=[dtag])
                        sqd = sqs[kind][h % 2]
                        TT(P, "dve", sqd[:, 512 * qc: 512 * qc + 512],
                           dst[:, 512 * qc: 512 * qc + 512], dst[:, 512 * qc: 512 * qc + 512],
                           ALU.mult, r=[dtag], w=[f"sqs{kind}{h % 2}"])
                    else:
                        dst = qhb_[hb] if kind == "q" else khb_[hb]
                        dtag = f"bh{kind}{hb}"
                        ACT(P, dst[:, 512 * qc: 512 * qc + 512], psp[:, :], AF.Copy, r=[pst], w=[dtag])

        def s2a(h):
            if not L0:
                return
            for kind in ("q", "k"):
                sqd = sqs[kind][h % 2]
                sqt = f"sqs{kind}{h % 2}"
                for qc in range(2):
                    if kind == "q":
                        pdx, pdt = (C.pd0, "pd0") if qc == 0 else (C.pd1, "pd1")
                    else:
                        pdx, pdt = (C.ps[4][0:1, :], "ps4") if qc == 0 else (C.ps[5][0:1, :], "ps5")
                    MM(P, pdx[0:1, 0:512], C.onesc, sqd[:, 512 * qc: 512 * qc + 512],
                       start=True, stop=True, r=[sqt, "onesc"], w=[pdt])

        def s2b(h):
            if not L0:
                return
            for kind in ("q", "k"):
                for qc in range(2):
                    if kind == "q":
                        pdx, pdt = (C.pd0, "pd0") if qc == 0 else (C.pd1, "pd1")
                    else:
                        pdx, pdt = (C.ps[4][0:1, :], "ps4") if qc == 0 else (C.ps[5][0:1, :], "ps5")
                    row = C.rowA if qc == 0 else C.rowB
                    rwt = "rowA" if qc == 0 else "rowB"
                    rowb = (C.rowAb if qc == 0 else C.rowBb) if kind == "q" else \
                           (C.rowCb if qc == 0 else C.rowDb)
                    rwbt = (("rowAb" if qc == 0 else "rowBb") if kind == "q" else
                            ("rowCb" if qc == 0 else "rowDb"))
                    ACT(P, row[0:1, 0:512], pdx[0:1, 0:512], AF.Ln, r=[pdt, "bcols"], w=[rwt],
                        bias=C.b_rms_eps, scale=1.0 / 128.0)
                    ACT(P, rowb[0:1, 0:512], row[0:1, 0:512], AF.Exp, r=[rwt], w=[rwbt],
                        scale=-0.5)

        def s2c(h):
            if not L0:
                return
            hb = h % 3
            for kind in ("q", "k"):
                src = rawq[hb] if kind == "q" else rawk[hb]
                stag = f"raw{kind}{hb}"
                for qc in range(2):
                    rowb = (C.rowAb if qc == 0 else C.rowBb) if kind == "q" else \
                           (C.rowCb if qc == 0 else C.rowDb)
                    rwbt = (("rowAb" if qc == 0 else "rowBb") if kind == "q" else
                            ("rowCb" if qc == 0 else "rowDb"))
                    psb = C.ps[4 + qc]
                    MM(P, psb[:, :], C.onesr, rowb[0:1, 0:512], start=True, stop=True,
                       r=[rwbt, "onesr"], w=[f"ps{4 + qc}"])
                    TT(P, "dve", src[:, 512 * qc: 512 * qc + 512], src[:, 512 * qc: 512 * qc + 512],
                       C.ps[4 + qc][:, :], ALU.mult, r=[stag, f"ps{4 + qc}"], w=[stag])

        def s3(h):
            hb = h % 3
            if L0:
                qsrc, qstagt, ksrc, kstagt = rawq[hb][:], f"rawq{hb}", rawk[hb][:], f"rawk{hb}"
            else:
                qsrc, qstagt, ksrc, kstagt = qhb_[hb][:], f"bhq{hb}", khb_[hb][:], f"bhk{hb}"
            rope(qsrc, qstagt, qstage[:, 1024 * h: 1024 * h + 1024], f"qs{h}", qtabs)
            rope(ksrc, kstagt, khro[0][:], "khro", ktabs)
            half, hh = h // 8, h % 8
            g4, h4 = h // 4, h % 4
            if h4 == 0 and h >= 4:
                g = (h - 4) // 4
                collective(f"cck{g}", C.cck_in[g], C.cck_out[g], cck_tags[g], f"ccoK{g}")
            stage(C.cck_in[g4][128 * h4: 128 * h4 + 128, :], khro[0][:], ["khro"],
                  cck_tags[g4], f"kstg{h % 2}")
            if h == NH - 1:
                collective("cck3", C.cck_in[3], C.cck_out[3], cck_tags[3], "ccoK3")

        s1(0)
        s1(1)
        for h in range(2, NH):
            load_head(h, h % 2)
            if h == 2:
                s2a(0)
                s2b(0)
                s2c(0)
            s2a(h - 1)
            s2b(h - 1)
            s1(h)
            s2c(h - 1)
            s3(h - 2)
        s2a(NH - 1)
        s2b(NH - 1)
        s2c(NH - 1)
        s3(NH - 2)
        s3(NH - 1)
        P.marks.append((f"attnA{layer}-end", P.n_array))
        C.adrain_tags = (["rt0", "rt1", "rt2", "rt3", "aq0", "aq1", "ak0", "ak1",
                          "av0h0", "av0h1", "av1h0", "av1h1", "khro", "rtmp",
                          "rawq0", "rawq1", "rawq2", "rawk0", "rawk1", "rawk2",
                          "bhq0", "bhq1", "bhq2", "bhk0", "bhk1", "bhk2",
                          "sqsq0", "sqsq1", "sqsk0", "sqsk1", "vtmp0", "vtmp1",
                          f"qs{NH - 1}"]
                         + ccv_tags[-4:] + [t for g in cck_tags for t in g[-1:]])

    with ExitStack() as sc:
        sb = lambda name, shape, dt: sc.enter_context(nc.sbuf_tensor(C.uniq(name), shape, dt))
        ptile = [sb(f"bt_p{i}", [128, 1024], BF16) for i in range(12)]
        kh = [sb(f"bt_k{i}", [128, 2048], BF16) for i in range(2)]
        vh = [sb(f"bt_v{i}", [128, 2048], BF16) for i in range(2)]
        bcf = [sb("bt_bc", [128, 512], F32)] * 2
        s2t = [sb("bt_s2", [128, 1024], BF16)] * 2 if layer == 1 else None
        osl = [sb(f"bt_o{i}", [128, NCH * 256], BF16) for i in range(2)]

        def load_kv(h, hb, early=False):
            half, hh = h // 4, h % 4
            ktag = f"ccoK{half}"
            ex = ["kvdrain"] if early else []
            for r_ in range(2):
                DMA(P, "pool", f"bk{hb}" + ("" if r_ == 0 else "b"),
                    kh[hb][:, 1024 * r_: 1024 * r_ + 1024],
                    C.cck_out[half][512 * r_ + 128 * hh: 512 * r_ + 128 * hh + 128, :],
                    r=[ktag, f"kh{hb}r{r_}"] + ex, w=[f"kh{hb}r{r_}"])
            for r_ in range(2):
                hi, d0 = h // 8, 128 * (h % 8)
                DMA(P, "pool", f"bv{hb}" + ("" if r_ == 0 else "b"),
                    vh[hb][:, 1024 * r_: 1024 * r_ + 1024].rearrange("p (c n) -> p c n", c=8),
                    C.ccv_out[2048 * r_ + hi: 2048 * (r_ + 1): 2, d0:d0 + 128]
                    .rearrange("(c p) d -> p c d", p=128),
                    r=["ccoV", f"vh{hb}r{r_}"] + ex, w=[f"vh{hb}r{r_}"])

        wop = C.wop0 if L0 else C.wop1

        def load_o(op_, b_, early=False):
            ex = ["adrain"] if early else []
            DMA(P, "pool", f"bo{b_}", osl[b_][:], wop[op_], r=[f"bo{b_}"] + ex, w=[f"bo{b_}"])

        P.op("pool", lambda e: e.nop(), reads=[],
             writes=["av0h0", "av0h1", "av1h0", "av1h1", "kvdrain"])
        load_kv(0, 0, early=True)
        load_kv(1, 1, early=True)
        P.op("pool", lambda e: e.nop(), reads=[], writes=C.adrain_tags + ["adrain"])
        load_o(0, 0, early=True)
        load_o(1, 1, early=True)
        P.barrier(exclude_slots=("ccv", "cck0", "cck1", "cck2", "cck3", "bk0", "bk1",
                                 "bk0b", "bk1b", "bv0", "bv1", "bv0b", "bv1b", "bo0", "bo1"))
        for h in range(NH):
            hb = h % 2
            if h + 1 < NH and h >= 1:
                load_kv(h + 1, (h + 1) % 2)
            for qhalf in range(2):
                u = 2 * h + qhalf
                npair = 4 if qhalf == 0 else 8
                ob = C.psw[2][:, 512 * (u % 2): 512 * (u % 2) + 512]
                obt = f"ps{4 + (u % 2)}"
                bcb = C.psw[2][:, 512 * ((u + 1) % 2): 512 * ((u + 1) % 2) + 512]
                bcbt = f"ps{4 + ((u + 1) % 2)}"
                dn, dnt = (C.pd0, "pd0") if qhalf == 0 else (C.pd1, "pd1")

                def pair_scores(n_):
                    i = n_
                    q0 = max(128 * i, 512 * qhalf)
                    w_ = 512 * (qhalf + 1) - q0
                    pp = n_ % 2
                    W, wt = C.psw[pp], [f"ps{2 * pp}", f"ps{2 * pp + 1}"]
                    MM(P, W[:, 0:w_], kh[hb][:, 128 * i: 128 * i + 128],
                       qstage[:, 1024 * h + q0: 1024 * h + q0 + w_],
                       start=True, stop=True, r=[f"kh{hb}r0", f"qs{h}"], w=[wt[0]])
                    MM(P, W[:, 512:512 + w_], kh[hb][:, 128 * (8 + i): 128 * (8 + i) + 128],
                       qstage[:, 1024 * h + q0: 1024 * h + q0 + w_],
                       start=True, stop=True, r=[f"kh{hb}r1", f"qs{h}"], w=[wt[1]])
                    pn = n_ if qhalf == 0 else 4 + n_
                    ptag = f"P{pn}"
                    Wg = W[:].rearrange("p (g n) -> p g n", g=2)[:, :, 0:w_]
                    Pg = ptile[pn][:].rearrange("p (g n) -> p g n", g=2)[:, :, 0:w_]
                    if L0:
                        ACT(P, Pg, Wg, AF.Exp, r=wt + ["bcols"],
                            w=[ptag], bias=C.b_negshift, scale=INV_SQRT_HD)
                    else:
                        Sg = s2t[pp][:].rearrange("p (g n) -> p g n", g=2)[:, :, 0:w_]
                        ACT(P, Sg, Wg, AF.Square, r=wt, w=["s2t"])
                        TT(P, "dve", Pg, Sg, Sg, ALU.mult, r=["s2t"], w=[ptag])
                    if qhalf == 0 or i >= 4:
                        TT(P, "dve",
                           ptile[pn][:].rearrange("p (g n) -> p g n", g=2)[:, :, 0:128],
                           ptile[pn][:].rearrange("p (g n) -> p g n", g=2)[:, :, 0:128],
                           C.m01_sb[:].rearrange("p (g n) -> p g n", g=2),
                           ALU.mult, r=[ptag, "m01"], w=[ptag])

                def pair_av(n_):
                    i = n_
                    q0 = max(128 * i, 512 * qhalf)
                    w_ = 512 * (qhalf + 1) - q0
                    off = q0 - 512 * qhalf
                    pn = n_ if qhalf == 0 else 4 + n_
                    ptag = f"P{pn}"
                    MM(P, ob[:, off:off + w_], vh[hb][:, 128 * i: 128 * i + 128],
                       ptile[pn][:, 0:w_], start=(n_ == 0), stop=False,
                       r=[f"vh{hb}r0", ptag], w=[obt])
                    MM(P, ob[:, off:off + w_], vh[hb][:, 128 * (8 + i): 128 * (8 + i) + 128],
                       ptile[pn][:, 512:512 + w_], start=False, stop=(n_ == npair - 1),
                       r=[f"vh{hb}r1", ptag], w=[obt])

                def pair_den(j):
                    q0 = max(128 * j, 512 * qhalf)
                    w_ = 512 * (qhalf + 1) - q0
                    off = q0 - 512 * qhalf
                    pn = j if qhalf == 0 else 4 + j
                    MM(P, dn[0:1, off:off + w_], C.onesc, ptile[pn][:, 0:w_],
                       start=(j == 0), stop=False, r=["onesc", f"P{pn}"], w=[dnt])
                    MM(P, dn[0:1, off:off + w_], C.onesc, ptile[pn][:, 512:512 + w_],
                       start=False, stop=(j == npair - 1), r=["onesc", f"P{pn}"], w=[dnt])

                for n_ in range(npair):
                    pair_scores(n_)
                    if n_ >= 2:
                        pair_av(n_ - 2)
                        pair_den(n_ - 2)
                pair_av(npair - 2)
                pair_den(npair - 2)
                pair_av(npair - 1)
                pair_den(npair - 1)
                rowx, rowxt = (C.rowC, "rowC") if qhalf == 0 else (C.rowD, "rowD")
                rowxb, rowxbt = (C.rowCb, "rowCb") if qhalf == 0 else (C.rowDb, "rowDb")
                TS(P, "dve", rowx[0:1, 0:512], dn[0:1, :], 1e-37 if L0 else 1e-6,
                   ALU.max, r=[dnt], w=[rowxt])
                ACT(P, rowx[0:1, 0:512], rowx[0:1, 0:512], AF.Ln, r=[rowxt], w=[rowxt])
                ACT(P, rowxb[0:1, 0:512], rowx[0:1, 0:512], AF.Exp, r=[rowxt], w=[rowxbt],
                    scale=-1.0)
                MM(P, bcb[:, :], C.onesr, rowxb[0:1, 0:512], start=True, stop=True,
                   r=[rowxbt, "onesr"], w=[bcbt])
                ACT(P, bcf[qhalf][:, :], bcb[:, :], AF.Copy, r=[bcbt], w=["bcf"])
                TT(P, "dve", C.xb[:, 1024 * h + 512 * qhalf: 1024 * h + 512 * qhalf + 512],
                   ob[:, :], bcf[qhalf][:, :], ALU.mult, r=[obt, "bcf"], w=[xbt(h, qhalf)])

        P.marks.append((f"attnB{layer}-scores-end", P.n_array))
        for op_ in range(8):
            b_ = op_ % 2
            for sub in range(2):
                oc = 2 * op_ + sub
                for qc in range(2):
                    pi = (4 * op_ + 2 * sub + qc) % 6
                    pso, psot = C.ps[pi], f"ps{pi}"
                    for c in range(NCH):
                        MM(P, pso[:, :], osl[b_][:, 256 * c + 128 * sub: 256 * c + 128 * sub + 128],
                           C.xb[:, 1024 * c + 512 * qc: 1024 * c + 512 * qc + 512],
                           start=(c == 0), stop=(c == NCH - 1), r=[f"bo{b_}", xbt(c, qc)], w=[psot])
                    xc = C.xf[:, 1024 * oc + 512 * qc: 1024 * oc + 512 * qc + 512]
                    TT(P, "dve", xc, pso[:, :], xc, ALU.add, r=[psot, xft(oc, qc)], w=[xft(oc, qc)])
            if op_ + 2 < 8:
                load_o(op_ + 2, b_)
        emit_ln(P, C, 0)
        emit_ln(P, C, 1)
        P.barrier()
    outer.close()


def emit_moe(P, C):
    nc = C.nc
    with ExitStack() as sc:
        sb = lambda name, shape, dt: sc.enter_context(nc.sbuf_tensor(C.uniq(name), shape, dt))
        acc = sb("mo_acc", [128, NCH * 512], F32)
        heb = sb("mo_he", [128, 11 * 512], BF16)
        wsa = sb("mo_wa", [128, NCH * 768], BF16)
        wsb = sb("mo_wb", [128, NCH * 768], BF16)
        rsl = sb("mo_rs", [128, 128], BF16)
        combm = sb("mo_cm", [128, 32], BF16)
        crow = sb("mo_cr", [1, 512], BF16)
        cbc = sb("mo_cbc", [128, 512], BF16)
        r1 = sb("mo_r1", [128, 1], F32)
        r2 = sb("mo_r2", [128, 1], F32)
        r3 = sb("mo_r3", [128, 8], F32)
        r4 = sb("mo_r4", [128, 8], F32)

        DMA(P, "pool", "mo_rs", rsl[:], C.wr1p, r=[], w=["rsl"])

        IH = [(0, 6, 768), (6, 5, 640)]
        slabs = [wsa, wsb]
        slab_t = ["wsa", "wsb"]

        def _mo_load(buf, dst_w, src):
            DMA(P, "pool", f"mo_{slab_t[buf]}", slabs[buf][:, 0:dst_w], src,
                r=[slab_t[buf]], w=[slab_t[buf]])

        def load_gu(which, e, ih, buf):
            i0, nch_, w_ = IH[ih]
            if which == "g":
                src = (C.wsgp0 if ih == 0 else C.wsgp1) if e == 8 else \
                      (C.wgep0 if ih == 0 else C.wgep1)[e]
            else:
                src = (C.wsup0 if ih == 0 else C.wsup1) if e == 8 else \
                      (C.wuep0 if ih == 0 else C.wuep1)[e]
            _mo_load(buf, NCH * w_, src)

        def load_de(e, oh, buf):
            src = C.wsdp[oh] if e == 8 else C.wdep[2 * e + oh]
            _mo_load(buf, 11 * 1024, src)

        for tc in range(2):
            t0 = 512 * tc
            load_gu("g", 0, 0, 0)
            load_gu("g", 0, 1, 1)
            for tq in range(4):
                psr, psrt = C.ps[tq % 2], f"ps{tq % 2}"
                for c in range(NCH):
                    MM(P, psr[:, 0:8], C.xb[:, 1024 * c + t0 + 128 * tq: 1024 * c + t0 + 128 * tq + 128],
                       rsl[:, 8 * c: 8 * c + 8], start=(c == 0), stop=(c == NCH - 1),
                       r=["rsl", xbt(c, tc)], w=[psrt])
                RED(P, r1[:, :], psr[:, 0:8], ALU.max, r=[psrt], w=["mor1"], negate=True)
                ACT(P, r3[:, :], psr[:, 0:8], AF.Exp, r=[psrt, "mor1"], w=["mor3"], bias=r1[:, :])
                RED(P, r2[:, :], r3[:, :], ALU.add, r=["mor3"], w=["mor2"])
                RECIP(P, r2[:, :], r2[:, :], r=["mor2"], w=["mor2"])
                TS(P, "dve", r3[:, :], r3[:, :], r2[:, :], ALU.mult, r=["mor3", "mor2"], w=["mor3"])
                RED(P, r1[:, :], r3[:, :], ALU.max, r=["mor3"], w=["mor1"])
                TS(P, "dve", r4[:, :], r3[:, :], r1[:, :], ALU.is_ge, r=["mor3", "mor1"], w=["mor4"])
                STT(P, "dve", r4[:, :], r4[:, :], -10.0, r3[:, :], ALU.mult, ALU.add, r=["mor4", "mor3"], w=["mor4"])
                RED(P, r1[:, :], r4[:, :], ALU.max, r=["mor4"], w=["mor1"])
                TS(P, "dve", r4[:, :], r3[:, :], r1[:, :], ALU.is_ge, r=["mor3", "mor1"], w=["mor4"])
                TT(P, "dve", combm[:, 8 * tq: 8 * tq + 8], r3[:, :], r4[:, :], ALU.mult,
                   r=["mor3", "mor4"], w=["combm"])

            for e in range(9):
                if e < 8:
                    for tq in range(4):
                        P.n_array += 1
                        if hasattr(P, "mm_labels"):
                            P.mm_labels.append(("combT", "bf16", 128))
                        P.op("pe", lambda eng, col=combm[:, 8 * tq + e: 8 * tq + e + 1],
                             o=C.ps[3][0:1, 0:64].bitcast(BF16), idn=C.ident_sb[:]:
                             eng.transpose(o, col, idn), reads=["combm", "ident"], writes=["ps3"])
                        ACT(P, crow[0:1, 128 * tq: 128 * tq + 128], C.ps[3][0:1, 0:64].bitcast(BF16),
                            AF.Copy, r=["ps3"], w=["crow"])
                    MM(P, C.ps[2][:, :], C.onesr, crow[0:1, :], start=True, stop=True,
                       r=["crow", "onesr"], w=["ps2"])
                    ACT(P, cbc[:, :], C.ps[2][:, :], AF.Copy, r=["ps2"], w=["cbc"])
                for ih in range(2):
                    buf = ih
                    i0, nch_, w_ = IH[ih]
                    for ic in range(nch_):
                        psg, psgt = C.ps[ic % 2], f"ps{ic % 2}"
                        for c in range(NCH):
                            MM(P, psg[:, :], slabs[buf][:, w_ * c + 128 * ic: w_ * c + 128 * ic + 128],
                               C.xb[:, 1024 * c + t0: 1024 * c + t0 + 512],
                               start=(c == 0), stop=(c == NCH - 1), r=[slab_t[buf], xbt(c, tc)], w=[psgt])
                        gic = i0 + ic
                        ACT(P, heb[:, 512 * gic: 512 * gic + 512], psg[:, :], AF.Sigmoid,
                            r=[psgt], w=[f"moh{gic}"])
                load_gu("u", e, 0, 0)
                load_gu("u", e, 1, 1)
                for ih in range(2):
                    buf = ih
                    i0, nch_, w_ = IH[ih]
                    for ic in range(nch_):
                        psg, psgt = C.ps[ic % 2], f"ps{ic % 2}"
                        for c in range(NCH):
                            MM(P, psg[:, :], slabs[buf][:, w_ * c + 128 * ic: w_ * c + 128 * ic + 128],
                               C.xb[:, 1024 * c + t0: 1024 * c + t0 + 512],
                               start=(c == 0), stop=(c == NCH - 1), r=[slab_t[buf], xbt(c, tc)], w=[psgt])
                        gic = i0 + ic
                        hslice = heb[:, 512 * gic: 512 * gic + 512]
                        TT(P, "dve", hslice, psg[:, :], hslice, ALU.mult, r=[psgt, f"moh{gic}"], w=[f"moh{gic}"])
                        if e < 8:
                            TT(P, "dve", hslice, hslice, cbc[:, :], ALU.mult,
                               r=[f"moh{gic}", "cbc"], w=[f"moh{gic}"])
                load_de(e, 0, 0)
                load_de(e, 1, 1)
                for oh in range(2):
                    buf = oh
                    for hq in range(8):
                        hc = 8 * oh + hq
                        pso, psot = C.ps[4 + (hq % 2)], f"ps{4 + (hq % 2)}"
                        for ic in range(11):
                            MM(P, pso[:, :], slabs[buf][:, 1024 * ic + 128 * hq: 1024 * ic + 128 * hq + 128],
                               heb[:, 512 * ic: 512 * ic + 512], start=(ic == 0), stop=(ic == 10),
                               r=[slab_t[buf], f"moh{ic}"], w=[psot])
                        ac = acc[:, 512 * hc: 512 * hc + 512]
                        if e == 0:
                            ACT(P, ac, pso[:, :], AF.Copy, r=[psot], w=[f"acc{hc}"])
                        else:
                            TT(P, "dve", ac, pso[:, :], ac, ALU.add, r=[psot, f"acc{hc}"], w=[f"acc{hc}"])
                if e + 1 < 9:
                    load_gu("g", e + 1, 0, 0)
                    load_gu("g", e + 1, 1, 1)
            for c in range(NCH):
                xc = C.xf[:, 1024 * c + t0: 1024 * c + t0 + 512]
                STT(P, "dve", xc, xc, 2.0, acc[:, 512 * c: 512 * c + 512], ALU.mult, ALU.add,
                    r=[xft(c, tc), f"acc{c}"], w=[xft(c, tc)])
            emit_ln(P, C, tc)
        P.barrier()


def emit_attn0(P, C):
    emit_attn(P, C, 0)


def emit_attn1(P, C):
    emit_attn(P, C, 1)



PHASE_INPUTS = {
    "ffn0": ["wgug", "wguu", "wdnp"],
    "attn0": ["wqp0", "wkp0", "wvp0", "wop0", "gq", "gk"],
    "attn1": ["wqp1", "wkp1", "wvp1", "wop1"],
    "moe1": ["wr1p", "wgep0", "wgep1", "wuep0", "wuep1",
             "wsgp0", "wsgp1", "wsup0", "wsup1", "wdep", "wsdp"],
}
CONST_INPUTS = ["cost", "sint", "m01", "ident", "onesc_bf", "onesr_bf", "onesr_f32", "bcols"]

SHAPES = {
    "xin": ([2048, 1024], F32),
    "gq": ([128, 1], F32), "gk": ([128, 1], F32),
    "wqp0": ([16, 128, 2048], F32), "wkp0": ([16, 128, 2048], F32),
    "wvp0": ([8, 128, 4096], F32), "wop0": ([8, 128, 4096], F32),
    "wqp1": ([16, 128, 2048], F32), "wkp1": ([16, 128, 2048], F32),
    "wvp1": ([8, 128, 4096], F32), "wop1": ([8, 128, 4096], F32),
    "wgug": ([32, 128, 4096], F32), "wguu": ([32, 128, 4096], F32),
    "wdnp": ([32, 128, 4096], F32),
    "wr1p": ([128, 128], F32),
    "wgep0": ([8, 128, 12288], F32), "wgep1": ([8, 128, 10240], F32),
    "wuep0": ([8, 128, 12288], F32), "wuep1": ([8, 128, 10240], F32),
    "wsgp0": ([128, 12288], F32), "wsgp1": ([128, 10240], F32),
    "wsup0": ([128, 12288], F32), "wsup1": ([128, 10240], F32),
    "wdep": ([16, 128, 11264], F32), "wsdp": ([2, 128, 11264], F32),
    "cost": ([64, 1024], F32), "sint": ([64, 1024], F32),
    "m01": ([128, 256], BF16),
    "ident": ([128, 128], BF16),
    "onesc_bf": ([128, 1], BF16), "onesr_bf": ([1, 128], BF16), "onesr_f32": ([1, 128], F32),
    "bcols": ([128, 4], F32),
}

PHASE_EMIT = {}


LAST_MARKS = []


def build_program(phases):
    nc = bass.Bass("TRN2", target_bir_lowering=False, debug=False)
    C = Ctx()
    C.nc = nc
    names = ["xin"] + CONST_INPUTS
    for ph in phases:
        for n in PHASE_INPUTS[ph]:
            if n not in names:
                names.append(n)
    for n in names:
        shape, dt = SHAPES[n]
        setattr(C, n, nc.dram_tensor(n, shape, dt, kind="ExternalInput").ap())
    C.xout = nc.dram_tensor("xout", [2048, 1024], F32, kind="ExternalOutput").ap()
    need_cc = any(ph in ("attn0", "attn1") for ph in phases)
    if need_cc:
        C.ccv_in = nc.dram_tensor("ccv_in", [2048, 1024], BF16, kind="Internal").ap()
        C.ccv_out = nc.dram_tensor("ccv_out", [4096, 1024], BF16, kind="Internal").ap()
        C.cck_in = [nc.dram_tensor(f"cck{i}_in", [512, 1024], BF16, kind="Internal").ap() for i in range(4)]
        C.cck_out = [nc.dram_tensor(f"cck{i}_out", [1024, 1024], BF16, kind="Internal").ap() for i in range(4)]

    with ExitStack() as ctx:
        sb = lambda name, shape, dt: ctx.enter_context(nc.sbuf_tensor(name, shape, dt))
        psal = lambda name, shape: ctx.enter_context(nc.psum_tensor(name, shape, F32))
        C.xf = sb("xf", [128, NCH * 1024], F32)[:]
        C.xb = sb("xb", [128, NCH * 1024], BF16)[:]
        C.sq = sb("sqscratch", [128, 512], BF16)[:]
        C.rowA = sb("rowA", [1, 512], F32)[:]
        C.rowB = sb("rowB", [1, 512], F32)[:]
        C.rowC = sb("rowC", [1, 512], F32)[:]
        C.rowD = sb("rowD", [1, 512], F32)[:]
        C.rowAb = sb("rowAb", [1, 512], BF16)[:]
        C.rowBb = sb("rowBb", [1, 512], BF16)[:]
        C.rowCb = sb("rowCb", [1, 512], BF16)[:]
        C.rowDb = sb("rowDb", [1, 512], BF16)[:]
        C.onesc_bf_sb = sb("onesc_sb", [128, 1], BF16)
        C.onesr_bf_sb = sb("onesr_sb", [1, 128], BF16)
        C.onesr_f32_sb = sb("onesr32_sb", [1, 128], F32)
        C.ident_sb = sb("ident_sb", [128, 128], BF16)
        C.m01_sb = sb("m01_sb", [128, 256], BF16)
        C.bcols_sb = sb("bcols_sb", [128, 4], F32)
        C.gq_sb = sb("gq_sb", [128, 1], F32)
        C.gk_sb = sb("gk_sb", [128, 1], F32)
        C.psw = [psal(f"psumw{i}", [128, 1024])[:] for i in range(3)]
        C.ps = [C.psw[i // 2][:, 512 * (i % 2): 512 * (i % 2) + 512] for i in range(6)]
        C.pd0 = psal("psumd0", [1, 512])[:]
        C.pd1 = psal("psumd1", [1, 512])[:]
        C.onesc = C.onesc_bf_sb[:]
        C.onesr = C.onesr_bf_sb[:]
        C.onesr32 = C.onesr_f32_sb[:]

        P = Prog(nc)
        C.P = P

        xin3 = _chunk3(C.xin)
        DMA(P, "sp", "xload", C.xf.rearrange("p (c n) -> p c n", c=NCH), xin3, r=[],
            w=[xft(c, h) for c in range(NCH) for h in range(2)])
        for g in range(4):
            cs_ = range(4 * g, 4 * g + 4)
            DMA(P, "pool", f"xbload{g}",
                C.xb[:, 4096 * g: 4096 * g + 4096].rearrange("p (c n) -> p c n", c=4),
                xin3[:, 4 * g: 4 * g + 4, :], r=[],
                w=[xbt(c, h) for c in cs_ for h in range(2)])
        DMA(P, "sp", "c_onesc", C.onesc_bf_sb[:], C.onesc_bf, r=[], w=["onesc"])
        DMA(P, "sp", "c_onesr", C.onesr_bf_sb[:], C.onesr_bf, r=[], w=["onesr"])
        DMA(P, "sp", "c_onesr32", C.onesr_f32_sb[:], C.onesr_f32, r=[], w=["onesr32"])
        DMA(P, "sp", "c_ident", C.ident_sb[:], C.ident, r=[], w=["ident"])
        DMA(P, "sp", "c_m01", C.m01_sb[:], C.m01, r=[], w=["m01"])
        DMA(P, "sp", "c_bcols", C.bcols_sb[:], C.bcols, r=[], w=["bcols"])
        C.b_ln_eps = C.bcols_sb[0:1, 0:1]
        C.b_rms_eps = C.bcols_sb[0:1, 1:2]
        C.b_negshift = C.bcols_sb[:, 2:3]
        if "attn0" in phases:
            DMA(P, "sp", "c_gq", C.gq_sb[:], C.gq, r=[], w=["gq"])
            DMA(P, "sp", "c_gk", C.gk_sb[:], C.gk, r=[], w=["gk"])

        for ph in phases:
            PHASE_EMIT[ph](P, C)
            P.marks.append((ph, P.n_array))
            P.barrier()

        DMA(P, "sp", "xstore", _chunk3(C.xout), C.xf.rearrange("p (c n) -> p c n", c=NCH),
            r=[xft(c, h) for c in range(NCH) for h in range(2)], w=["xoutdone"])
        P.op("sp", lambda e: e.nop(), reads=["xoutdone"], writes=[])

        block = ctx.enter_context(nc.Block())
        P.finalize(block, ctx)
        LAST_MARKS[:] = P.marks
        globals()["LAST_PROG"] = P
    return nc, names



SEGMENTS = [["attn0", "ffn0", "attn1", "moe1"]]


def kernel(**inputs):
    shared, percore = _host_prep(inputs)
    stream = [percore[c]["xin"] for c in range(NCORES)]
    for phases in SEGMENTS:
        nc, names = build_program(phases)
        in_maps = []
        for c in range(NCORES):
            m = {}
            for n in names:
                if n == "xin":
                    m[n] = stream[c]
                elif n in shared:
                    m[n] = shared[n]
                else:
                    m[n] = percore[c][n]
            in_maps.append(m)
        res = run_bass_kernel_spmd(nc, in_maps, core_ids=list(range(NCORES)))
        stream = [res.results[c]["xout"] for c in range(NCORES)]
    return _assemble_output(res.results)


PHASE_EMIT["ffn0"] = emit_ffn0
PHASE_EMIT["attn0"] = emit_attn0
PHASE_EMIT["attn1"] = emit_attn1
PHASE_EMIT["moe1"] = emit_moe


# version 26
# speedup vs baseline: 1.0896x; 1.0896x over previous
import numpy as np
import ml_dtypes
from contextlib import ExitStack

import concourse.bass as bass
import concourse.mybir as mybir
from concourse.bass_utils import run_bass_kernel_spmd

F32 = mybir.dt.float32
BF16 = mybir.dt.bfloat16
AF = mybir.ActivationFunctionType
ALU = mybir.AluOpType
AXX = mybir.AxisListType.X

B, S, DM = 4, 2048, 2048
NH, HD, NCH = 16, 128, 16
T = 1024
NCORES = 8
INV_SQRT_HD = float(1.0 / np.sqrt(128.0))
EXP_SHIFT = 32.0

BF16NP = ml_dtypes.bfloat16



def _own_blocks(rank):
    return [2 * j + rank for j in range(8)]


def _own_positions(rank):
    return np.concatenate([np.arange(128 * g, 128 * (g + 1)) for g in _own_blocks(rank)])


def _pack128(w2d):
    return np.ascontiguousarray(w2d.reshape(16, 128, 16, 128).transpose(2, 1, 0, 3).reshape(16, 128, 2048))


def _pack256(w2d):
    return np.ascontiguousarray(w2d.reshape(16, 128, 8, 256).transpose(2, 1, 0, 3).reshape(8, 128, 4096))


def _host_prep(inputs):
    shared = {}
    tr = lambda a: np.ascontiguousarray(np.asarray(a).T)
    wqkv = np.asarray(inputs["L0_attn_w_qkv"])
    shared["wqp0"] = _pack128(wqkv[:, 0:2048])
    shared["wkp0"] = _pack128(wqkv[:, 2048:4096])
    shared["wvp0"] = _pack256(wqkv[:, 4096:6144])
    shared["wop0"] = _pack256(np.asarray(inputs["L0_attn_w_out"]))
    shared["gq"] = np.asarray(inputs["L0_attn_q_gamma"]).reshape(128, 1).copy()
    shared["gk"] = np.asarray(inputs["L0_attn_k_gamma"]).reshape(128, 1).copy()
    wgu = tr(inputs["L0_ffn_w_gate_up"])
    shared["wgug"] = np.ascontiguousarray(
        wgu[:, 0:8192].reshape(16, 128, 32, 2, 128).transpose(2, 1, 3, 0, 4).reshape(32, 128, 4096))
    shared["wguu"] = np.ascontiguousarray(
        wgu[:, 8192:16384].reshape(16, 128, 32, 2, 128).transpose(2, 1, 3, 0, 4).reshape(32, 128, 4096))
    wdn = tr(inputs["L0_ffn_w_down"])
    shared["wdnp"] = np.ascontiguousarray(
        wdn.reshape(8, 8, 128, 4, 512).transpose(0, 3, 2, 1, 4).reshape(32, 128, 4096))
    shared["wqp1"] = _pack128(tr(inputs["L1_attn_Wq"]))
    shared["wkp1"] = _pack128(tr(inputs["L1_attn_Wk"]))
    shared["wvp1"] = _pack256(tr(inputs["L1_attn_Wv"]))
    shared["wop1"] = _pack256(tr(inputs["L1_attn_Wo"]))
    shared["wr1p"] = np.ascontiguousarray(
        tr(inputs["L1_ffn_router_w"]).reshape(16, 128, 8).transpose(1, 0, 2).reshape(128, 128))
    wsg, wsu = tr(inputs["L1_ffn_shared_gate"]), tr(inputs["L1_ffn_shared_up"])
    for nm, w_ in (("wsgp", wsg), ("wsup", wsu)):
        a = w_.reshape(16, 128, 1408)
        shared[nm + "0"] = np.ascontiguousarray(a[:, :, 0:768].transpose(1, 0, 2).reshape(128, 12288))
        shared[nm + "1"] = np.ascontiguousarray(a[:, :, 768:1408].transpose(1, 0, 2).reshape(128, 10240))
    shared["wsdp"] = np.ascontiguousarray(
        tr(inputs["L1_ffn_shared_down"]).reshape(11, 128, 2, 1024).transpose(2, 1, 0, 3).reshape(2, 128, 11264))
    wge = np.asarray(inputs["L1_ffn_w_gate"]).transpose(0, 2, 1)
    wue = np.asarray(inputs["L1_ffn_w_up"]).transpose(0, 2, 1)
    for nm, w_ in (("wgep", wge), ("wuep", wue)):
        a = w_.reshape(8, 16, 128, 1408)
        shared[nm + "0"] = np.ascontiguousarray(a[:, :, :, 0:768].transpose(0, 2, 1, 3).reshape(8, 128, 12288))
        shared[nm + "1"] = np.ascontiguousarray(a[:, :, :, 768:1408].transpose(0, 2, 1, 3).reshape(8, 128, 10240))
    wde = np.asarray(inputs["L1_ffn_w_down"]).transpose(0, 2, 1)
    shared["wdep"] = np.ascontiguousarray(
        wde.reshape(8, 11, 128, 2, 1024).transpose(0, 3, 2, 1, 4).reshape(16, 128, 11264))

    ident = np.eye(128, dtype=BF16NP)
    tri = (np.arange(128)[None, :] >= np.arange(128)[:, None]).astype(BF16NP)
    onesc_bf = np.ones((128, 1), dtype=BF16NP)
    onesr_bf = np.ones((1, 128), dtype=BF16NP)
    onesr_f32 = np.ones((1, 128), dtype=np.float32)
    zeros = np.zeros((128, 128), dtype=BF16NP)
    ones128 = np.ones((128, 128), dtype=BF16NP)

    x = np.asarray(inputs["L0_attn_x"])
    percore = []
    for c in range(NCORES):
        b, rank = divmod(c, 2)
        pos = _own_positions(rank)
        d = {}
        d["xin"] = np.ascontiguousarray(x[b].T[:, pos])
        inv_freq = (10000.0 ** (-np.arange(0, 128, 2, dtype=np.float32) / 128.0))
        ang = pos.astype(np.float32)[None, :] * inv_freq[:, None]
        d["cost"] = np.cos(ang).astype(np.float32)
        d["sint"] = np.sin(ang).astype(np.float32)
        d["m0"] = tri if rank == 0 else ones128
        d["m1"] = zeros if rank == 0 else tri
        d["m01"] = np.concatenate([d["m0"], d["m1"]], axis=1)
        d["ident"] = ident
        d["onesc_bf"] = onesc_bf
        d["onesr_bf"] = onesr_bf
        d["onesr_f32"] = onesr_f32
        d["bcols"] = np.repeat(np.array([[1e-5, 1e-6, -EXP_SHIFT, 0.0]], dtype=np.float32), 128, axis=0)
        percore.append(d)
    return shared, percore


def _assemble_output(results):
    out = np.empty((B, S, DM), dtype=np.float32)
    for c in range(NCORES):
        b, rank = divmod(c, 2)
        pos = _own_positions(rank)
        out[b, pos, :] = results[c]["xout"].T
    return out



ENGINES = ("pe", "act", "dve", "pool", "sp")
_BLOCK_ATTR = {"pe": "tensor", "act": "scalar", "dve": "vector", "pool": "gpsimd", "sp": "sync"}


class Prog:
    def __init__(self, nc):
        self.nc = nc
        self.ops = {e: [] for e in ENGINES}
        self.lw = {}
        self.rd = {}
        self.slots = {}
        self.slot_inc = {}
        self.last_e = {}
        self.n_array = 0
        self.marks = []

    def _dep_events(self, eng, reads, writes):
        deps = []
        for t in reads:
            ev = self.lw.get(t)
            if ev is not None:
                deps.append(ev)
        for t in writes:
            ev = self.lw.get(t)
            if ev is not None:
                deps.append(ev)
            deps.extend(self.rd.get(t, ()))
        ded = {}
        for ev in deps:
            if ev[0] == "e" and ev[1] == eng and eng in ("pe", "sp"):
                continue
            key = (ev[0], ev[1])
            if key not in ded or ded[key][2] < ev[2]:
                ded[key] = ev
        return list(ded.values())

    def _record(self, myev, reads, writes):
        for t in reads:
            self.rd.setdefault(t, []).append(myev)
        for t in writes:
            self.lw[t] = myev
            self.rd[t] = []

    def op(self, eng, fn, reads=(), writes=()):
        waits = self._dep_events(eng, reads, writes)
        idx = len(self.ops[eng])
        myev = ("e", eng, idx)
        self.ops[eng].append(dict(fn=fn, waits=waits, marked=False, slot=None, inc=0))
        self.last_e[eng] = idx
        self._record(myev, reads, writes)

    def dma(self, eng, slot, fn, reads=(), writes=(), inc=16):
        waits = self._dep_events(eng, reads, writes)
        cnt = self.slots.get(slot, 0) + 1
        self.slots[slot] = cnt
        self.slot_inc[slot] = inc
        myev = ("d", slot, cnt)
        self.ops[eng].append(dict(fn=fn, waits=waits, marked=False, slot=slot, inc=inc))
        self._record(myev, reads, writes)

    def barrier(self, exclude_slots=()):
        evs = []
        for e in ENGINES:
            if e in self.last_e:
                evs.append(("e", e, self.last_e[e]))
        for s, c in self.slots.items():
            if s in exclude_slots:
                continue
            evs.append(("d", s, c))
        for e in ENGINES:
            ded = [ev for ev in evs if not (ev[0] == "e" and ev[1] == e)]
            idx = len(self.ops[e])
            self.ops[e].append(dict(fn=lambda eng: eng.nop(nofuse=True), waits=ded, marked=False, slot=None, inc=0))
        keep = {t: ev for t, ev in self.lw.items()
                if ev[0] == "d" and ev[1] in exclude_slots}
        self.lw = keep
        self.rd = {t: [] for t in keep}

    def finalize(self, block, ctx):
        nc = self.nc
        sems = {}
        for e in ENGINES:
            sems[e] = ctx.enter_context(nc.semaphore(f"sem_{e}"))
        for s in self.slots:
            sems[s] = ctx.enter_context(nc.semaphore(f"sem_{s}"))
        for e in ENGINES:
            for op in self.ops[e]:
                for ev in op["waits"]:
                    if ev[0] == "e":
                        self.ops[ev[1]][ev[2]]["marked"] = True
        rank = {}
        for e in ENGINES:
            c, r = 0, []
            for op in self.ops[e]:
                if op["marked"]:
                    c += 1
                r.append(c)
            rank[e] = r

        def mk_runner(ename):
            ops = self.ops[ename]

            def run(eng):
                waited = {}
                for op in ops:
                    for ev in op["waits"]:
                        if ev[0] == "e":
                            key, val = ev[1], rank[ev[1]][ev[2]]
                        else:
                            key, val = ev[1], self.slot_inc[ev[1]] * ev[2]
                        if waited.get(key, -1) >= val:
                            continue
                        waited[key] = val
                        eng.wait_ge(sems[key], val)
                    inst = op["fn"](eng)
                    if op["slot"] is not None:
                        inst.then_inc(sems[op["slot"]], op["inc"]) if op["inc"] != 1 else inst.then_inc(sems[op["slot"]])
                    elif op["marked"]:
                        inst.then_inc(sems[ename], 1)
            return run

        for e in ENGINES:
            getattr(block, _BLOCK_ATTR[e])(mk_runner(e))



def MM(P, out_ap, lhsT_ap, rhs_ap, start, stop, r, w):
    P.n_array += 1
    if hasattr(P, "mm_labels"):
        P.mm_labels.append(("+".join(r[:2]), str(rhs_ap.tensor.dtype), rhs_ap.shape[-1] if hasattr(rhs_ap, "shape") else 0))
    P.op("pe", lambda e, o=out_ap, l=lhsT_ap, rr=rhs_ap, s1=start, s2=stop:
         e.matmul(o, l, rr, start=s1, stop=s2), reads=r, writes=w)


def ACT(P, out_ap, in_ap, func, r, w, bias=0.0, scale=1.0):
    P.op("act", lambda e, o=out_ap, i=in_ap, f=func, b=bias, s=scale:
         e.activation(o, i, f, bias=b, scale=s), reads=r, writes=w)


def TT(P, eng, out_ap, a_ap, b_ap, alu, r, w):
    P.op(eng, lambda e, o=out_ap, a=a_ap, b=b_ap, op=alu:
         e.tensor_tensor(o, a, b, op), reads=r, writes=w)


def TS(P, eng, out_ap, in_ap, scalar, alu, r, w):
    P.op(eng, lambda e, o=out_ap, i=in_ap, s=scalar, op=alu:
         e.tensor_single_scalar(o, i, s, op), reads=r, writes=w)


def STT(P, eng, out_ap, in0_ap, scalar, in1_ap, op0, op1, r, w):
    P.op(eng, lambda e, o=out_ap, i0=in0_ap, s=scalar, i1=in1_ap, a=op0, b=op1:
         e.scalar_tensor_tensor(o, i0, s, i1, a, b), reads=r, writes=w)


def RED(P, out_ap, in_ap, alu, r, w, negate=False):
    P.op("dve", lambda e, o=out_ap, i=in_ap, op=alu, n=negate:
         e.tensor_reduce(o, i, AXX, op, negate=n), reads=r, writes=w)


def RECIP(P, out_ap, in_ap, r, w):
    P.op("dve", lambda e, o=out_ap, i=in_ap: e.reciprocal(o, i), reads=r, writes=w)


def RECIPB(P, out_ap, in_ap, r, w):
    nc = P.nc

    def fn(e, o=out_ap, i=in_ap):
        with nc.allow_low_precision(reason="bf16 broadcast row (within tolerance model)"):
            return e.reciprocal(o, i)
    P.op("dve", fn, reads=r, writes=w)


def DMA(P, eng, slot, out_ap, in_ap, r, w):
    P.dma(eng, slot, lambda e, o=out_ap, i=in_ap: e.dma_start(out=o, in_=i), reads=r, writes=w)



class Ctx:
    def __init__(self):
        self._uid = 0

    def uniq(self, name):
        self._uid += 1
        return f"{name}_{self._uid}"


def _chunk3(ap, p=128):
    return ap.rearrange("(c p) n -> p c n", p=p)


def xft(c, h):
    return f"xf{c}_{h}"


def xbt(c, h):
    return f"xb{c}_{h}"


def emit_ln(P, C, tc, write_xb=True):
    xf, xb = C.xf, C.xb
    t0 = 512 * tc
    for c in range(NCH):
        src = xf[:, 1024 * c + t0: 1024 * c + t0 + 512]
        dst = xb[:, 1024 * c + t0: 1024 * c + t0 + 512]
        ACT(P, dst, src, AF.Copy, r=[xft(c, tc)], w=[xbt(c, tc)])
        ACT(P, C.sq[:, 0:512], dst, AF.Square, r=[xbt(c, tc)], w=["sq"])
        MM(P, C.pd0[0:1, :], C.onesc, dst, start=(c == 0), stop=(c == NCH - 1),
           r=[xbt(c, tc), "onesc"], w=["pd0"])
        MM(P, C.pd1[0:1, :], C.onesc, C.sq[:, 0:512], start=(c == 0), stop=(c == NCH - 1),
           r=["sq", "onesc"], w=["pd1"])
    rA, rB, rC, rD = C.rowA[0:1, 0:512], C.rowB[0:1, 0:512], C.rowC[0:1, 0:512], C.rowD[0:1, 0:512]
    TS(P, "dve", rA, C.pd0[0:1, :], 1.0 / 2048.0, ALU.mult, r=["pd0"], w=["rowA"])
    ACT(P, rB, rA, AF.Square, r=["rowA"], w=["rowB"])
    STT(P, "dve", rC, C.pd1[0:1, :], 1.0 / 2048.0, rB, ALU.mult, ALU.subtract, r=["pd1", "rowB"], w=["rowC"])
    ACT(P, rB, rC, AF.Ln, r=["rowC", "bcols"], w=["rowB"], bias=C.b_ln_eps)
    ACT(P, rB, rB, AF.Exp, r=["rowB"], w=["rowB"], scale=-0.5)
    TT(P, "dve", rD, rA, rB, ALU.mult, r=["rowA", "rowB"], w=["rowD"])
    MM(P, C.ps[4][:, :], C.onesr32, rB, start=True, stop=True, r=["rowB", "onesr32"], w=["ps4"])
    MM(P, C.ps[5][:, :], C.onesr32, rD, start=True, stop=True, r=["rowD", "onesr32"], w=["ps5"])
    for c in range(NCH):
        xc = xf[:, 1024 * c + t0: 1024 * c + t0 + 512]
        TT(P, "dve", xc, xc, C.ps[4][:, :], ALU.mult, r=[xft(c, tc), "ps4"], w=[xft(c, tc)])
        TT(P, "dve", xc, xc, C.ps[5][:, :], ALU.subtract, r=[xft(c, tc), "ps5"], w=[xft(c, tc)])
        if write_xb:
            dst = xb[:, 1024 * c + t0: 1024 * c + t0 + 512]
            ACT(P, dst, xc, AF.Copy, r=[xft(c, tc)], w=[xbt(c, tc)])


def emit_ffn0(P, C):
    nc = C.nc
    with ExitStack() as sc:
        sb = lambda name, shape, dt: sc.enter_context(nc.sbuf_tensor(C.uniq(name), shape, dt))
        gsl = [sb(f"ffn_g{i}", [128, NCH * 256], BF16) for i in range(2)]
        usl = [sb(f"ffn_u{i}", [128, NCH * 256], BF16) for i in range(2)]
        dsl = [sb(f"ffn_d{i}", [128, 8 * 512], BF16) for i in range(2)]
        hblk = [sb(f"ffn_h{i}", [128, 8 * 512], BF16) for i in range(2)]
        acc = sb("ffn_acc", [128, NCH * 512], F32)
        sg = sb("ffn_sg", [128, 512], BF16)

        def load_gu(pair, buf, split=True, early=False):
            ex = ["ffdrain"] if early else []
            for sub in (range(2) if split else (None,)):
                sl_ = slice(0, 4096) if sub is None else slice(2048 * sub, 2048 * sub + 2048)
                gt = [f"gsl{buf}s0", f"gsl{buf}s1"] if sub is None else [f"gsl{buf}s{sub}"]
                ut = [f"usl{buf}s0", f"usl{buf}s1"] if sub is None else [f"usl{buf}s{sub}"]
                sfx = "" if not sub else "b"
                DMA(P, "pool", f"wg{buf}{sfx}", gsl[buf][:, sl_], C.wgug[pair][:, sl_],
                    r=gt + ex, w=gt)
                DMA(P, "pool", f"wu{buf}{sfx}", usl[buf][:, sl_], C.wguu[pair][:, sl_],
                    r=ut + ex, w=ut)

        def load_dn(ib, ocq, buf):
            for hf in range(2):
                sl_ = slice(2048 * hf, 2048 * hf + 2048)
                DMA(P, "pool", f"wd{buf}" + ("" if hf == 0 else "b"),
                    dsl[buf][:, sl_], C.wdnp[4 * ib + ocq][:, sl_],
                    r=[f"dsl{buf}"], w=[f"dsl{buf}"])

        P.op("pool", lambda e: e.nop(), reads=[], writes=C.prev_tags + ["ffdrain"])
        load_gu(0, 0, split=True, early=True)
        P.barrier(exclude_slots=("wg0", "wu0", "wg0b", "wu0b"))

        for tc in range(2):
            t0 = 512 * tc

            def gate_pair(ib, pair):
                hb = ib % 2
                if True:
                    buf = pair % 2
                    if pair + 1 < 32:
                        load_gu(pair + 1, (pair + 1) % 2)
                    elif tc == 0:
                        load_gu(0, 0)
                    for sub in range(2):
                        oc = 2 * pair + sub
                        lo = oc - 8 * ib
                        psg, psu = C.ps[2 * sub], C.ps[2 * sub + 1]
                        for c in range(NCH):
                            lg = gsl[buf][:, 2048 * sub + 128 * c: 2048 * sub + 128 * c + 128]
                            lu = usl[buf][:, 2048 * sub + 128 * c: 2048 * sub + 128 * c + 128]
                            rhs = C.xb[:, 1024 * c + t0: 1024 * c + t0 + 512]
                            MM(P, psg[:, :], lg, rhs, start=(c == 0), stop=(c == NCH - 1),
                               r=[f"gsl{buf}s{sub}", xbt(c, tc)], w=[f"ps{2*sub}"])
                            MM(P, psu[:, :], lu, rhs, start=(c == 0), stop=(c == NCH - 1),
                               r=[f"usl{buf}s{sub}", xbt(c, tc)], w=[f"ps{2*sub+1}"])
                        hc = hblk[hb][:, 512 * lo: 512 * lo + 512]
                        ACT(P, sg[:, :], psg[:, :], AF.Silu, r=[f"ps{2*sub}"], w=["ffnsg"])
                        TT(P, "dve", hc, psu[:, :], sg[:, :], ALU.mult,
                           r=[f"ps{2*sub+1}", "ffnsg"], w=[f"h{hb}_{lo}"])

            def down_quarter(ib, ocq):
                hb = ib % 2
                if True:
                    buf = (ib + ocq) % 2
                    if ocq + 1 < 4:
                        load_dn(ib, ocq + 1, (ib + ocq + 1) % 2)
                    for hq in range(4):
                        hc = 4 * ocq + hq
                        pso = C.ps[4 + (hc % 2)]
                        for ic in range(8):
                            MM(P, pso[:, :], dsl[buf][:, 512 * ic + 128 * hq: 512 * ic + 128 * hq + 128],
                               hblk[hb][:, 512 * ic: 512 * ic + 512], start=(ic == 0), stop=(ic == 7),
                               r=[f"dsl{buf}", f"h{hb}_{ic}"], w=[f"ps{4 + (hc % 2)}"])
                        ac = acc[:, 512 * hc: 512 * hc + 512]
                        if ib == 0:
                            ACT(P, ac, pso[:, :], AF.Copy, r=[f"ps{4 + (hc % 2)}"], w=[f"acc{hc}"])
                        else:
                            TT(P, "dve", ac, pso[:, :], ac, ALU.add,
                               r=[f"ps{4 + (hc % 2)}", f"acc{hc}"], w=[f"acc{hc}"])

            for pair in range(4):
                gate_pair(0, pair)
            for ib in range(1, 8):
                load_dn(ib - 1, 0, (ib - 1) % 2)
                for pair in range(4):
                    gate_pair(ib, 4 * ib + pair)
                    down_quarter(ib - 1, pair)
            load_dn(7, 0, 1)
            for pair in range(4):
                down_quarter(7, pair)
            for c in range(NCH):
                xc = C.xf[:, 1024 * c + t0: 1024 * c + t0 + 512]
                TT(P, "dve", xc, acc[:, 512 * c: 512 * c + 512], xc, ALU.add,
                   r=[f"acc{c}", xft(c, tc)], w=[xft(c, tc)])
            emit_ln(P, C, tc)
        C.prev_tags = (["dsl0", "dsl1"] + [f"h{p}_{lo}" for p in range(2) for lo in range(8)]
                       + [f"acc{i}" for i in range(NCH)])


def emit_attn(P, C, layer):
    nc = C.nc
    L0 = layer == 0

    outer = ExitStack()
    qstage_h = outer.enter_context(nc.sbuf_tensor(C.uniq("at_qstage"), [128, NH * 1024], BF16))
    qstage = qstage_h[:]
    with ExitStack() as sc:
        sb = lambda name, shape, dt: sc.enter_context(nc.sbuf_tensor(C.uniq(name), shape, dt))
        rtab = sb("at_rtab", [128, 4 * 1024], BF16)
        qsl = [sb(f"at_q{i}", [128, NCH * 128], BF16) for i in range(2)]
        ksl = [sb(f"at_k{i}", [128, NCH * 128], BF16) for i in range(2)]
        vsl = [sb(f"at_v{i}", [128, NCH * 256], BF16) for i in range(2)]

        wqp = C.wqp0 if L0 else C.wqp1
        wkp = C.wkp0 if L0 else C.wkp1
        wvp = C.wvp0 if L0 else C.wvp1

        def load_head(h, hb, early=False):
            ex = ["atdrain"] if early else []
            DMA(P, "pool", f"aq{hb}", qsl[hb][:], wqp[h], r=[f"aq{hb}"] + ex, w=[f"aq{hb}"])
            DMA(P, "pool", f"ak{hb}", ksl[hb][:], wkp[h], r=[f"ak{hb}"] + ex, w=[f"ak{hb}"])

        def load_v(s, early=False):
            b = s % 2
            ex = ["atdrain"] if early else []
            for hf in range(2):
                DMA(P, "pool", f"av{b}" + ("" if hf == 0 else "b"),
                    vsl[b][:, 2048 * hf: 2048 * hf + 2048],
                    wvp[s][:, 2048 * hf: 2048 * hf + 2048],
                    r=[f"av{b}h{hf}"] + ex, w=[f"av{b}h{hf}"])

        P.op("pool", lambda e: e.nop(), reads=[], writes=C.prev_tags + ["atdrain"])
        load_head(0, 0, early=True)
        load_head(1, 1, early=True)
        load_v(0, early=True)
        P.barrier(exclude_slots=("aq0", "aq1", "ak0", "ak1", "av0", "av0b"))

        tbl_sc = ExitStack()
        tsb = lambda name, shape, dt: tbl_sc.enter_context(nc.sbuf_tensor(C.uniq(name), shape, dt))
        cosf = tsb("at_cosf", [64, 1024], F32)
        sinf = tsb("at_sinf", [64, 1024], F32)
        scr64 = tsb("at_scr64", [64, 1024], BF16)
        gcol = tsb("at_gcol", [64, 3], F32)
        DMA(P, "sp", "c_cos", cosf[:], C.cost, r=["cos"], w=["cos"])
        DMA(P, "sp", "c_sin", sinf[:], C.sint, r=["sin"], w=["sin"])
        rt = rtab[:]

        def build_tables(ci, si, gam_sb, gtag):
            Ct = rt[:, 1024 * ci: 1024 * ci + 1024]
            St = rt[:, 1024 * si: 1024 * si + 1024]
            if gam_sb is None:
                P.op("dve", lambda e: e.tensor_copy(Ct[0:64, :], cosf[:]), reads=["cos"], writes=[f"rt{ci}"])
                P.op("dve", lambda e: e.tensor_copy(scr64[:], cosf[:]), reads=["cos"], writes=["scr64"])
                P.op("dve", lambda e: e.tensor_copy(Ct[64:128, :], scr64[:]), reads=["scr64"], writes=[f"rt{ci}"])
                TS(P, "dve", St[0:64, :], sinf[:], -1.0, ALU.mult, r=["sin"], w=[f"rt{si}"])
                P.op("dve", lambda e: e.tensor_copy(scr64[:], sinf[:]), reads=["sin"], writes=["scr64"])
                P.op("dve", lambda e: e.tensor_copy(St[64:128, :], scr64[:]), reads=["scr64"], writes=[f"rt{si}"])
                return
            gt0 = gam_sb[0:64, :]
            gb0 = gcol[:, 0:1]
            gb0n = gcol[:, 1:2]
            P.op("dve", lambda e: e.tensor_copy(gb0, gam_sb[64:128, :]), reads=[gtag], writes=["gcol"])
            TS(P, "dve", gb0n, gb0, -1.0, ALU.mult, r=["gcol"], w=["gcol"])
            TS(P, "dve", Ct[0:64, :], cosf[:], gt0, ALU.mult, r=["cos", gtag], w=[f"rt{ci}"])
            TS(P, "dve", scr64[:], cosf[:], gb0, ALU.mult, r=["cos", "gcol"], w=["scr64"])
            P.op("dve", lambda e: e.tensor_copy(Ct[64:128, :], scr64[:]), reads=["scr64"], writes=[f"rt{ci}"])
            TS(P, "dve", St[0:64, :], sinf[:], gb0n, ALU.mult, r=["sin", "gcol"], w=[f"rt{si}"])
            TS(P, "dve", scr64[:], sinf[:], gt0, ALU.mult, r=["sin", gtag], w=["scr64"])
            P.op("dve", lambda e: e.tensor_copy(St[64:128, :], scr64[:]), reads=["scr64"], writes=[f"rt{si}"])

        if L0:
            build_tables(0, 1, C.gq_sb, "gq")
            build_tables(2, 3, C.gk_sb, "gk")
            qtabs, ktabs = (0, 1), (2, 3)
        else:
            build_tables(0, 1, None, None)
            qtabs = ktabs = (0, 1)
        tbl_sc.close()
        P.barrier(exclude_slots=("ccv", "cck0", "cck1", "cck2", "cck3"))

        khro = [sb("at_khro", [128, 1024], BF16)] * 2
        rtmp = sb("at_rtmp", [128, 1024], BF16)
        if L0:
            sqs = {"q": [sb(f"at_sqq{i}", [128, 1024], BF16) for i in range(2)],
                   "k": [sb(f"at_sqk{i}", [128, 1024], BF16) for i in range(2)]}
        vtmp = [sb(f"at_vtmp{i}", [128, 256], BF16) for i in range(2)]
        rawq3 = [sb(f"at_rq{i}", [128, 1024], BF16) for i in range(3)] if L0 else None
        rawk3 = [sb(f"at_rk{i}", [128, 1024], BF16) for i in range(3)] if L0 else None
        qhb3 = [sb(f"at_qhb{i}", [128, 1024], BF16) for i in range(3)] if not L0 else None
        khb3 = [sb(f"at_khb{i}", [128, 1024], BF16) for i in range(3)] if not L0 else None

        def rope(src_ap, src_tag, dst_ap, dst_tag, tabs):
            ci, si = tabs
            Ct = rt[:, 1024 * ci: 1024 * ci + 1024]
            St = rt[:, 1024 * si: 1024 * si + 1024]
            ACT(P, rtmp[0:64, :], src_ap[64:128, :], AF.Copy, r=[src_tag], w=["rtmp"])
            ACT(P, rtmp[64:128, :], src_ap[0:64, :], AF.Copy, r=[src_tag], w=["rtmp"])
            TT(P, "dve", rtmp[:], rtmp[:], St, ALU.mult, r=["rtmp", f"rt{si}"], w=["rtmp"])
            TT(P, "dve", dst_ap, src_ap, Ct, ALU.mult, r=[src_tag, f"rt{ci}"], w=[dst_tag])
            TT(P, "dve", dst_ap, dst_ap, rtmp[:], ALU.add, r=[dst_tag, "rtmp"], w=[dst_tag])

        cctag = [0]

        def stage(out_ap, in_ap, rtags, acc, sl):
            t = f"ccin{cctag[0]}"
            cctag[0] += 1
            DMA(P, "sp", sl, out_ap, in_ap, r=rtags + [t], w=[t])
            acc.append(t)

        groups = [[2 * i, 2 * i + 1] for i in range(4)]

        def collective(slot, in_ap, out_ap, rtags, outtag):
            P.dma("pool", slot,
                  lambda e, i=in_ap, o=out_ap: e.collective_compute(
                      "AllGather", ALU.bypass, replica_groups=groups, ins=[i.opt()], outs=[o.opt()]),
                  reads=rtags, writes=[outtag], inc=1)

        ccv_tags = []
        cck_tags = [[] for _ in range(4)]

        for s in range(8):
            if s + 1 < 8:
                load_v(s + 1)
            sb_ = s % 2
            for tc in range(8):
                psv = C.ps[4 + tc % 2]
                pst = f"ps{4 + tc % 2}"
                for c in range(NCH):
                    MM(P, psv[:, 0:256], C.xb[:, 1024 * c + 128 * tc: 1024 * c + 128 * tc + 128],
                       vsl[sb_][:, 256 * c: 256 * c + 256], start=(c == 0), stop=(c == NCH - 1),
                       r=[f"av{sb_}h{c // 8}", xbt(c, tc // 4)], w=[pst])
                k_ = 8 * s + tc
                vb = k_ % 2
                ACT(P, vtmp[vb][:, 0:256], psv[:, 0:256], AF.Copy, r=[pst], w=[f"vtmp{vb}"])
                hi, d0 = s // 4, 256 * (s % 4)
                stage(C.ccv_in[256 * tc + hi: 256 * (tc + 1): 2, d0:d0 + 256],
                      vtmp[vb][:, 0:256], [f"vtmp{vb}"], ccv_tags, f"vstg{k_ % 4}")
        collective("ccv", C.ccv_in, C.ccv_out, ccv_tags, "ccoV")

        P.marks.append((f"attnA{layer}-Vdone", P.n_array))
        rawq = rawq3
        rawk = rawk3
        qhb_ = qhb3
        khb_ = khb3

        def s1(h):
            hb = h % 3
            wb = h % 2
            for kind in ("q", "k"):
                sl = qsl[wb] if kind == "q" else ksl[wb]
                stag = f"a{kind}{wb}"
                for qc in range(2):
                    psp = C.ps[qc if kind == "q" else 2 + qc]
                    pst = f"ps{qc if kind == 'q' else 2 + qc}"
                    for c in range(NCH):
                        MM(P, psp[:, :], sl[:, 128 * c: 128 * c + 128],
                           C.xb[:, 1024 * c + 512 * qc: 1024 * c + 512 * qc + 512],
                           start=(c == 0), stop=(c == NCH - 1), r=[stag, xbt(c, qc)], w=[pst])
                    if L0:
                        dst = rawq[hb] if kind == "q" else rawk[hb]
                        dtag = f"raw{kind}{hb}"
                        ACT(P, dst[:, 512 * qc: 512 * qc + 512], psp[:, :], AF.Copy, r=[pst], w=[dtag])
                        sqd = sqs[kind][h % 2]
                        TT(P, "dve", sqd[:, 512 * qc: 512 * qc + 512],
                           dst[:, 512 * qc: 512 * qc + 512], dst[:, 512 * qc: 512 * qc + 512],
                           ALU.mult, r=[dtag], w=[f"sqs{kind}{h % 2}"])
                    else:
                        dst = qhb_[hb] if kind == "q" else khb_[hb]
                        dtag = f"bh{kind}{hb}"
                        ACT(P, dst[:, 512 * qc: 512 * qc + 512], psp[:, :], AF.Copy, r=[pst], w=[dtag])

        def s2a(h):
            if not L0:
                return
            for kind in ("q", "k"):
                sqd = sqs[kind][h % 2]
                sqt = f"sqs{kind}{h % 2}"
                for qc in range(2):
                    if kind == "q":
                        pdx, pdt = (C.pd0, "pd0") if qc == 0 else (C.pd1, "pd1")
                    else:
                        pdx, pdt = (C.ps[4][0:1, :], "ps4") if qc == 0 else (C.ps[5][0:1, :], "ps5")
                    MM(P, pdx[0:1, 0:512], C.onesc, sqd[:, 512 * qc: 512 * qc + 512],
                       start=True, stop=True, r=[sqt, "onesc"], w=[pdt])

        def s2b(h):
            if not L0:
                return
            for kind in ("q", "k"):
                for qc in range(2):
                    if kind == "q":
                        pdx, pdt = (C.pd0, "pd0") if qc == 0 else (C.pd1, "pd1")
                    else:
                        pdx, pdt = (C.ps[4][0:1, :], "ps4") if qc == 0 else (C.ps[5][0:1, :], "ps5")
                    row = C.rowA if qc == 0 else C.rowB
                    rwt = "rowA" if qc == 0 else "rowB"
                    rowb = (C.rowAb if qc == 0 else C.rowBb) if kind == "q" else \
                           (C.rowCb if qc == 0 else C.rowDb)
                    rwbt = (("rowAb" if qc == 0 else "rowBb") if kind == "q" else
                            ("rowCb" if qc == 0 else "rowDb"))
                    ACT(P, row[0:1, 0:512], pdx[0:1, 0:512], AF.Ln, r=[pdt, "bcols"], w=[rwt],
                        bias=C.b_rms_eps, scale=1.0 / 128.0)
                    ACT(P, rowb[0:1, 0:512], row[0:1, 0:512], AF.Exp, r=[rwt], w=[rwbt],
                        scale=-0.5)

        def s2c(h):
            if not L0:
                return
            hb = h % 3
            for kind in ("q", "k"):
                src = rawq[hb] if kind == "q" else rawk[hb]
                stag = f"raw{kind}{hb}"
                for qc in range(2):
                    rowb = (C.rowAb if qc == 0 else C.rowBb) if kind == "q" else \
                           (C.rowCb if qc == 0 else C.rowDb)
                    rwbt = (("rowAb" if qc == 0 else "rowBb") if kind == "q" else
                            ("rowCb" if qc == 0 else "rowDb"))
                    psb = C.ps[4 + qc]
                    MM(P, psb[:, :], C.onesr, rowb[0:1, 0:512], start=True, stop=True,
                       r=[rwbt, "onesr"], w=[f"ps{4 + qc}"])
                    TT(P, "dve", src[:, 512 * qc: 512 * qc + 512], src[:, 512 * qc: 512 * qc + 512],
                       C.ps[4 + qc][:, :], ALU.mult, r=[stag, f"ps{4 + qc}"], w=[stag])

        def s3(h):
            hb = h % 3
            if L0:
                qsrc, qstagt, ksrc, kstagt = rawq[hb][:], f"rawq{hb}", rawk[hb][:], f"rawk{hb}"
            else:
                qsrc, qstagt, ksrc, kstagt = qhb_[hb][:], f"bhq{hb}", khb_[hb][:], f"bhk{hb}"
            rope(qsrc, qstagt, qstage[:, 1024 * h: 1024 * h + 1024], f"qs{h}", qtabs)
            rope(ksrc, kstagt, khro[0][:], "khro", ktabs)
            half, hh = h // 8, h % 8
            g4, h4 = h // 4, h % 4
            if h4 == 0 and h >= 4:
                g = (h - 4) // 4
                collective(f"cck{g}", C.cck_in[g], C.cck_out[g], cck_tags[g], f"ccoK{g}")
            stage(C.cck_in[g4][128 * h4: 128 * h4 + 128, :], khro[0][:], ["khro"],
                  cck_tags[g4], f"kstg{h % 2}")
            if h == NH - 1:
                collective("cck3", C.cck_in[3], C.cck_out[3], cck_tags[3], "ccoK3")

        s1(0)
        s1(1)
        for h in range(2, NH):
            load_head(h, h % 2)
            if h == 2:
                s2a(0)
                s2b(0)
                s2c(0)
            s2a(h - 1)
            s2b(h - 1)
            s1(h)
            s2c(h - 1)
            s3(h - 2)
        s2a(NH - 1)
        s2b(NH - 1)
        s2c(NH - 1)
        s3(NH - 2)
        s3(NH - 1)
        P.marks.append((f"attnA{layer}-end", P.n_array))
        C.adrain_tags = (["rt0", "rt1", "rt2", "rt3", "aq0", "aq1", "ak0", "ak1",
                          "av0h0", "av0h1", "av1h0", "av1h1", "khro", "rtmp",
                          "rawq0", "rawq1", "rawq2", "rawk0", "rawk1", "rawk2",
                          "bhq0", "bhq1", "bhq2", "bhk0", "bhk1", "bhk2",
                          "sqsq0", "sqsq1", "sqsk0", "sqsk1", "vtmp0", "vtmp1",
                          f"qs{NH - 1}"]
                         + ccv_tags[-4:] + [t for g in cck_tags for t in g[-1:]])

    with ExitStack() as sc:
        sb = lambda name, shape, dt: sc.enter_context(nc.sbuf_tensor(C.uniq(name), shape, dt))
        ptile = [sb(f"bt_p{i}", [128, 1024], BF16) for i in range(12)]
        kh = [sb(f"bt_k{i}", [128, 2048], BF16) for i in range(2)]
        vh = [sb(f"bt_v{i}", [128, 2048], BF16) for i in range(2)]
        bcf = [sb("bt_bc", [128, 512], F32)] * 2
        s2t = [sb("bt_s2", [128, 1024], BF16)] * 2 if layer == 1 else None
        osl = [sb(f"bt_o{i}", [128, NCH * 256], BF16) for i in range(2)]

        def load_kv(h, hb, early=False):
            half, hh = h // 4, h % 4
            ktag = f"ccoK{half}"
            ex = ["kvdrain"] if early else []
            for r_ in range(2):
                DMA(P, "pool", f"bk{hb}" + ("" if r_ == 0 else "b"),
                    kh[hb][:, 1024 * r_: 1024 * r_ + 1024],
                    C.cck_out[half][512 * r_ + 128 * hh: 512 * r_ + 128 * hh + 128, :],
                    r=[ktag, f"kh{hb}r{r_}"] + ex, w=[f"kh{hb}r{r_}"])
            for r_ in range(2):
                hi, d0 = h // 8, 128 * (h % 8)
                DMA(P, "pool", f"bv{hb}" + ("" if r_ == 0 else "b"),
                    vh[hb][:, 1024 * r_: 1024 * r_ + 1024].rearrange("p (c n) -> p c n", c=8),
                    C.ccv_out[2048 * r_ + hi: 2048 * (r_ + 1): 2, d0:d0 + 128]
                    .rearrange("(c p) d -> p c d", p=128),
                    r=["ccoV", f"vh{hb}r{r_}"] + ex, w=[f"vh{hb}r{r_}"])

        wop = C.wop0 if L0 else C.wop1

        def load_o(op_, b_, early=False):
            ex = ["adrain"] if early else []
            DMA(P, "pool", f"bo{b_}", osl[b_][:], wop[op_], r=[f"bo{b_}"] + ex, w=[f"bo{b_}"])

        P.op("pool", lambda e: e.nop(), reads=[],
             writes=["av0h0", "av0h1", "av1h0", "av1h1", "kvdrain"])
        load_kv(0, 0, early=True)
        load_kv(1, 1, early=True)
        P.op("pool", lambda e: e.nop(), reads=[], writes=C.adrain_tags + ["adrain"])
        load_o(0, 0, early=True)
        load_o(1, 1, early=True)
        P.barrier(exclude_slots=("ccv", "cck0", "cck1", "cck2", "cck3", "bk0", "bk1",
                                 "bk0b", "bk1b", "bv0", "bv1", "bv0b", "bv1b", "bo0", "bo1"))
        for h in range(NH):
            hb = h % 2
            if h + 1 < NH and h >= 1:
                load_kv(h + 1, (h + 1) % 2)
            for qhalf in range(2):
                u = 2 * h + qhalf
                npair = 4 if qhalf == 0 else 8
                ob = C.psw[2][:, 512 * (u % 2): 512 * (u % 2) + 512]
                obt = f"ps{4 + (u % 2)}"
                bcb = C.psw[2][:, 512 * ((u + 1) % 2): 512 * ((u + 1) % 2) + 512]
                bcbt = f"ps{4 + ((u + 1) % 2)}"
                dn, dnt = (C.pd0, "pd0") if qhalf == 0 else (C.pd1, "pd1")

                def pair_scores(n_):
                    i = n_
                    q0 = max(128 * i, 512 * qhalf)
                    w_ = 512 * (qhalf + 1) - q0
                    pp = n_ % 2
                    W, wt = C.psw[pp], [f"ps{2 * pp}", f"ps{2 * pp + 1}"]
                    MM(P, W[:, 0:w_], kh[hb][:, 128 * i: 128 * i + 128],
                       qstage[:, 1024 * h + q0: 1024 * h + q0 + w_],
                       start=True, stop=True, r=[f"kh{hb}r0", f"qs{h}"], w=[wt[0]])
                    MM(P, W[:, 512:512 + w_], kh[hb][:, 128 * (8 + i): 128 * (8 + i) + 128],
                       qstage[:, 1024 * h + q0: 1024 * h + q0 + w_],
                       start=True, stop=True, r=[f"kh{hb}r1", f"qs{h}"], w=[wt[1]])
                    pn = n_ if qhalf == 0 else 4 + n_
                    ptag = f"P{pn}"
                    Wg = W[:].rearrange("p (g n) -> p g n", g=2)[:, :, 0:w_]
                    Pg = ptile[pn][:].rearrange("p (g n) -> p g n", g=2)[:, :, 0:w_]
                    if L0:
                        ACT(P, Pg, Wg, AF.Exp, r=wt + ["bcols"],
                            w=[ptag], bias=C.b_negshift, scale=INV_SQRT_HD)
                    else:
                        Sg = s2t[pp][:].rearrange("p (g n) -> p g n", g=2)[:, :, 0:w_]
                        ACT(P, Sg, Wg, AF.Square, r=wt, w=["s2t"])
                        TT(P, "dve", Pg, Sg, Sg, ALU.mult, r=["s2t"], w=[ptag])
                    if qhalf == 0 or i >= 4:
                        TT(P, "dve",
                           ptile[pn][:].rearrange("p (g n) -> p g n", g=2)[:, :, 0:128],
                           ptile[pn][:].rearrange("p (g n) -> p g n", g=2)[:, :, 0:128],
                           C.m01_sb[:].rearrange("p (g n) -> p g n", g=2),
                           ALU.mult, r=[ptag, "m01"], w=[ptag])

                def pair_av(n_):
                    i = n_
                    q0 = max(128 * i, 512 * qhalf)
                    w_ = 512 * (qhalf + 1) - q0
                    off = q0 - 512 * qhalf
                    pn = n_ if qhalf == 0 else 4 + n_
                    ptag = f"P{pn}"
                    MM(P, ob[:, off:off + w_], vh[hb][:, 128 * i: 128 * i + 128],
                       ptile[pn][:, 0:w_], start=(n_ == 0), stop=False,
                       r=[f"vh{hb}r0", ptag], w=[obt])
                    MM(P, ob[:, off:off + w_], vh[hb][:, 128 * (8 + i): 128 * (8 + i) + 128],
                       ptile[pn][:, 512:512 + w_], start=False, stop=(n_ == npair - 1),
                       r=[f"vh{hb}r1", ptag], w=[obt])

                def pair_den(j):
                    q0 = max(128 * j, 512 * qhalf)
                    w_ = 512 * (qhalf + 1) - q0
                    off = q0 - 512 * qhalf
                    pn = j if qhalf == 0 else 4 + j
                    MM(P, dn[0:1, off:off + w_], C.onesc, ptile[pn][:, 0:w_],
                       start=(j == 0), stop=False, r=["onesc", f"P{pn}"], w=[dnt])
                    MM(P, dn[0:1, off:off + w_], C.onesc, ptile[pn][:, 512:512 + w_],
                       start=False, stop=(j == npair - 1), r=["onesc", f"P{pn}"], w=[dnt])

                for n_ in range(npair):
                    pair_scores(n_)
                    if n_ >= 2:
                        pair_av(n_ - 2)
                        pair_den(n_ - 2)
                pair_av(npair - 2)
                pair_den(npair - 2)
                pair_av(npair - 1)
                pair_den(npair - 1)
                rowx, rowxt = (C.rowC, "rowC") if qhalf == 0 else (C.rowD, "rowD")
                rowxb, rowxbt = (C.rowCb, "rowCb") if qhalf == 0 else (C.rowDb, "rowDb")
                TS(P, "dve", rowx[0:1, 0:512], dn[0:1, :], 1e-37 if L0 else 1e-6,
                   ALU.max, r=[dnt], w=[rowxt])
                ACT(P, rowx[0:1, 0:512], rowx[0:1, 0:512], AF.Ln, r=[rowxt], w=[rowxt])
                ACT(P, rowxb[0:1, 0:512], rowx[0:1, 0:512], AF.Exp, r=[rowxt], w=[rowxbt],
                    scale=-1.0)
                MM(P, bcb[:, :], C.onesr, rowxb[0:1, 0:512], start=True, stop=True,
                   r=[rowxbt, "onesr"], w=[bcbt])
                ACT(P, bcf[qhalf][:, :], bcb[:, :], AF.Copy, r=[bcbt], w=["bcf"])
                TT(P, "dve", C.xb[:, 1024 * h + 512 * qhalf: 1024 * h + 512 * qhalf + 512],
                   ob[:, :], bcf[qhalf][:, :], ALU.mult, r=[obt, "bcf"], w=[xbt(h, qhalf)])

        P.marks.append((f"attnB{layer}-scores-end", P.n_array))
        for op_ in range(8):
            b_ = op_ % 2
            for sub in range(2):
                oc = 2 * op_ + sub
                for qc in range(2):
                    pi = (4 * op_ + 2 * sub + qc) % 6
                    pso, psot = C.ps[pi], f"ps{pi}"
                    for c in range(NCH):
                        MM(P, pso[:, :], osl[b_][:, 256 * c + 128 * sub: 256 * c + 128 * sub + 128],
                           C.xb[:, 1024 * c + 512 * qc: 1024 * c + 512 * qc + 512],
                           start=(c == 0), stop=(c == NCH - 1), r=[f"bo{b_}", xbt(c, qc)], w=[psot])
                    xc = C.xf[:, 1024 * oc + 512 * qc: 1024 * oc + 512 * qc + 512]
                    TT(P, "dve", xc, pso[:, :], xc, ALU.add, r=[psot, xft(oc, qc)], w=[xft(oc, qc)])
            if op_ + 2 < 8:
                load_o(op_ + 2, b_)
        emit_ln(P, C, 0)
        emit_ln(P, C, 1)
        C.prev_tags = [f"qs{h}" for h in range(NH)]
    outer.close()


def emit_moe(P, C):
    nc = C.nc
    with ExitStack() as sc:
        sb = lambda name, shape, dt: sc.enter_context(nc.sbuf_tensor(C.uniq(name), shape, dt))
        wsa = sb("mo_wa", [128, NCH * 768], BF16)
        acc = sb("mo_acc", [128, NCH * 512], F32)
        heb = sb("mo_he", [128, 11 * 512], BF16)
        wsb = sb("mo_wb", [128, NCH * 768], BF16)
        rsl = sb("mo_rs", [128, 128], BF16)
        combm = sb("mo_cm", [128, 32], BF16)
        crow = sb("mo_cr", [1, 512], BF16)
        cbc = sb("mo_cbc", [128, 512], BF16)
        r1 = sb("mo_r1", [128, 1], F32)
        r2 = sb("mo_r2", [128, 1], F32)
        r3 = sb("mo_r3", [128, 8], F32)
        r4 = sb("mo_r4", [128, 8], F32)

        IH = [(0, 6, 768), (6, 5, 640)]
        slabs = [wsa, wsb]
        slab_t = ["wsa", "wsb"]

        def _mo_load(buf, dst_w, src, early=False):
            ex = ["modrain"] if early else []
            DMA(P, "pool", f"mo_{slab_t[buf]}", slabs[buf][:, 0:dst_w], src,
                r=[slab_t[buf]] + ex, w=[slab_t[buf]])

        def load_gu(which, e, ih, buf, early=False):
            i0, nch_, w_ = IH[ih]
            if which == "g":
                src = (C.wsgp0 if ih == 0 else C.wsgp1) if e == 8 else \
                      (C.wgep0 if ih == 0 else C.wgep1)[e]
            else:
                src = (C.wsup0 if ih == 0 else C.wsup1) if e == 8 else \
                      (C.wuep0 if ih == 0 else C.wuep1)[e]
            _mo_load(buf, NCH * w_, src, early=early)

        P.op("pool", lambda e: e.nop(), reads=[], writes=C.prev_tags + ["modrain"])
        load_gu("g", 0, 0, 0, early=True)
        P.barrier(exclude_slots=("mo_wsa",))
        DMA(P, "pool", "mo_rs", rsl[:], C.wr1p, r=[], w=["rsl"])

        def load_de(e, oh, buf):
            src = C.wsdp[oh] if e == 8 else C.wdep[2 * e + oh]
            _mo_load(buf, 11 * 1024, src)

        for tc in range(2):
            t0 = 512 * tc
            if tc == 1:
                load_gu("g", 0, 0, 0)
            load_gu("g", 0, 1, 1)
            for tq in range(4):
                psr, psrt = C.ps[tq % 2], f"ps{tq % 2}"
                for c in range(NCH):
                    MM(P, psr[:, 0:8], C.xb[:, 1024 * c + t0 + 128 * tq: 1024 * c + t0 + 128 * tq + 128],
                       rsl[:, 8 * c: 8 * c + 8], start=(c == 0), stop=(c == NCH - 1),
                       r=["rsl", xbt(c, tc)], w=[psrt])
                RED(P, r1[:, :], psr[:, 0:8], ALU.max, r=[psrt], w=["mor1"], negate=True)
                ACT(P, r3[:, :], psr[:, 0:8], AF.Exp, r=[psrt, "mor1"], w=["mor3"], bias=r1[:, :])
                RED(P, r2[:, :], r3[:, :], ALU.add, r=["mor3"], w=["mor2"])
                RECIP(P, r2[:, :], r2[:, :], r=["mor2"], w=["mor2"])
                TS(P, "dve", r3[:, :], r3[:, :], r2[:, :], ALU.mult, r=["mor3", "mor2"], w=["mor3"])
                RED(P, r1[:, :], r3[:, :], ALU.max, r=["mor3"], w=["mor1"])
                TS(P, "dve", r4[:, :], r3[:, :], r1[:, :], ALU.is_ge, r=["mor3", "mor1"], w=["mor4"])
                STT(P, "dve", r4[:, :], r4[:, :], -10.0, r3[:, :], ALU.mult, ALU.add, r=["mor4", "mor3"], w=["mor4"])
                RED(P, r1[:, :], r4[:, :], ALU.max, r=["mor4"], w=["mor1"])
                TS(P, "dve", r4[:, :], r3[:, :], r1[:, :], ALU.is_ge, r=["mor3", "mor1"], w=["mor4"])
                TT(P, "dve", combm[:, 8 * tq: 8 * tq + 8], r3[:, :], r4[:, :], ALU.mult,
                   r=["mor3", "mor4"], w=["combm"])

            for e in range(9):
                if e < 8:
                    for tq in range(4):
                        P.n_array += 1
                        if hasattr(P, "mm_labels"):
                            P.mm_labels.append(("combT", "bf16", 128))
                        P.op("pe", lambda eng, col=combm[:, 8 * tq + e: 8 * tq + e + 1],
                             o=C.ps[3][0:1, 0:64].bitcast(BF16), idn=C.ident_sb[:]:
                             eng.transpose(o, col, idn), reads=["combm", "ident"], writes=["ps3"])
                        ACT(P, crow[0:1, 128 * tq: 128 * tq + 128], C.ps[3][0:1, 0:64].bitcast(BF16),
                            AF.Copy, r=["ps3"], w=["crow"])
                    MM(P, C.ps[2][:, :], C.onesr, crow[0:1, :], start=True, stop=True,
                       r=["crow", "onesr"], w=["ps2"])
                    ACT(P, cbc[:, :], C.ps[2][:, :], AF.Copy, r=["ps2"], w=["cbc"])
                for ih in range(2):
                    buf = ih
                    i0, nch_, w_ = IH[ih]
                    for ic in range(nch_):
                        psg, psgt = C.ps[ic % 2], f"ps{ic % 2}"
                        for c in range(NCH):
                            MM(P, psg[:, :], slabs[buf][:, w_ * c + 128 * ic: w_ * c + 128 * ic + 128],
                               C.xb[:, 1024 * c + t0: 1024 * c + t0 + 512],
                               start=(c == 0), stop=(c == NCH - 1), r=[slab_t[buf], xbt(c, tc)], w=[psgt])
                        gic = i0 + ic
                        ACT(P, heb[:, 512 * gic: 512 * gic + 512], psg[:, :], AF.Sigmoid,
                            r=[psgt], w=[f"moh{gic}"])
                load_gu("u", e, 0, 0)
                load_gu("u", e, 1, 1)
                for ih in range(2):
                    buf = ih
                    i0, nch_, w_ = IH[ih]
                    for ic in range(nch_):
                        psg, psgt = C.ps[ic % 2], f"ps{ic % 2}"
                        for c in range(NCH):
                            MM(P, psg[:, :], slabs[buf][:, w_ * c + 128 * ic: w_ * c + 128 * ic + 128],
                               C.xb[:, 1024 * c + t0: 1024 * c + t0 + 512],
                               start=(c == 0), stop=(c == NCH - 1), r=[slab_t[buf], xbt(c, tc)], w=[psgt])
                        gic = i0 + ic
                        hslice = heb[:, 512 * gic: 512 * gic + 512]
                        TT(P, "dve", hslice, psg[:, :], hslice, ALU.mult, r=[psgt, f"moh{gic}"], w=[f"moh{gic}"])
                        if e < 8:
                            TT(P, "dve", hslice, hslice, cbc[:, :], ALU.mult,
                               r=[f"moh{gic}", "cbc"], w=[f"moh{gic}"])
                load_de(e, 0, 0)
                load_de(e, 1, 1)
                for oh in range(2):
                    buf = oh
                    for hq in range(8):
                        hc = 8 * oh + hq
                        pso, psot = C.ps[4 + (hq % 2)], f"ps{4 + (hq % 2)}"
                        for ic in range(11):
                            MM(P, pso[:, :], slabs[buf][:, 1024 * ic + 128 * hq: 1024 * ic + 128 * hq + 128],
                               heb[:, 512 * ic: 512 * ic + 512], start=(ic == 0), stop=(ic == 10),
                               r=[slab_t[buf], f"moh{ic}"], w=[psot])
                        ac = acc[:, 512 * hc: 512 * hc + 512]
                        if e == 0:
                            ACT(P, ac, pso[:, :], AF.Copy, r=[psot], w=[f"acc{hc}"])
                        else:
                            TT(P, "dve", ac, pso[:, :], ac, ALU.add, r=[psot, f"acc{hc}"], w=[f"acc{hc}"])
                if e + 1 < 9:
                    load_gu("g", e + 1, 0, 0)
                    load_gu("g", e + 1, 1, 1)
            for c in range(NCH):
                xc = C.xf[:, 1024 * c + t0: 1024 * c + t0 + 512]
                STT(P, "dve", xc, xc, 2.0, acc[:, 512 * c: 512 * c + 512], ALU.mult, ALU.add,
                    r=[xft(c, tc), f"acc{c}"], w=[xft(c, tc)])
            emit_ln(P, C, tc)


def emit_attn0(P, C):
    emit_attn(P, C, 0)


def emit_attn1(P, C):
    emit_attn(P, C, 1)



PHASE_INPUTS = {
    "ffn0": ["wgug", "wguu", "wdnp"],
    "attn0": ["wqp0", "wkp0", "wvp0", "wop0", "gq", "gk"],
    "attn1": ["wqp1", "wkp1", "wvp1", "wop1"],
    "moe1": ["wr1p", "wgep0", "wgep1", "wuep0", "wuep1",
             "wsgp0", "wsgp1", "wsup0", "wsup1", "wdep", "wsdp"],
}
CONST_INPUTS = ["cost", "sint", "m01", "ident", "onesc_bf", "onesr_bf", "onesr_f32", "bcols"]

SHAPES = {
    "xin": ([2048, 1024], F32),
    "gq": ([128, 1], F32), "gk": ([128, 1], F32),
    "wqp0": ([16, 128, 2048], F32), "wkp0": ([16, 128, 2048], F32),
    "wvp0": ([8, 128, 4096], F32), "wop0": ([8, 128, 4096], F32),
    "wqp1": ([16, 128, 2048], F32), "wkp1": ([16, 128, 2048], F32),
    "wvp1": ([8, 128, 4096], F32), "wop1": ([8, 128, 4096], F32),
    "wgug": ([32, 128, 4096], F32), "wguu": ([32, 128, 4096], F32),
    "wdnp": ([32, 128, 4096], F32),
    "wr1p": ([128, 128], F32),
    "wgep0": ([8, 128, 12288], F32), "wgep1": ([8, 128, 10240], F32),
    "wuep0": ([8, 128, 12288], F32), "wuep1": ([8, 128, 10240], F32),
    "wsgp0": ([128, 12288], F32), "wsgp1": ([128, 10240], F32),
    "wsup0": ([128, 12288], F32), "wsup1": ([128, 10240], F32),
    "wdep": ([16, 128, 11264], F32), "wsdp": ([2, 128, 11264], F32),
    "cost": ([64, 1024], F32), "sint": ([64, 1024], F32),
    "m01": ([128, 256], BF16),
    "ident": ([128, 128], BF16),
    "onesc_bf": ([128, 1], BF16), "onesr_bf": ([1, 128], BF16), "onesr_f32": ([1, 128], F32),
    "bcols": ([128, 4], F32),
}

PHASE_EMIT = {}


LAST_MARKS = []


def build_program(phases):
    nc = bass.Bass("TRN2", target_bir_lowering=False, debug=False)
    C = Ctx()
    C.nc = nc
    names = ["xin"] + CONST_INPUTS
    for ph in phases:
        for n in PHASE_INPUTS[ph]:
            if n not in names:
                names.append(n)
    for n in names:
        shape, dt = SHAPES[n]
        setattr(C, n, nc.dram_tensor(n, shape, dt, kind="ExternalInput").ap())
    C.xout = nc.dram_tensor("xout", [2048, 1024], F32, kind="ExternalOutput").ap()
    need_cc = any(ph in ("attn0", "attn1") for ph in phases)
    if need_cc:
        C.ccv_in = nc.dram_tensor("ccv_in", [2048, 1024], BF16, kind="Internal").ap()
        C.ccv_out = nc.dram_tensor("ccv_out", [4096, 1024], BF16, kind="Internal").ap()
        C.cck_in = [nc.dram_tensor(f"cck{i}_in", [512, 1024], BF16, kind="Internal").ap() for i in range(4)]
        C.cck_out = [nc.dram_tensor(f"cck{i}_out", [1024, 1024], BF16, kind="Internal").ap() for i in range(4)]

    with ExitStack() as ctx:
        sb = lambda name, shape, dt: ctx.enter_context(nc.sbuf_tensor(name, shape, dt))
        psal = lambda name, shape: ctx.enter_context(nc.psum_tensor(name, shape, F32))
        C.xf = sb("xf", [128, NCH * 1024], F32)[:]
        C.xb = sb("xb", [128, NCH * 1024], BF16)[:]
        C.sq = sb("sqscratch", [128, 512], BF16)[:]
        C.rowA = sb("rowA", [1, 512], F32)[:]
        C.rowB = sb("rowB", [1, 512], F32)[:]
        C.rowC = sb("rowC", [1, 512], F32)[:]
        C.rowD = sb("rowD", [1, 512], F32)[:]
        C.rowAb = sb("rowAb", [1, 512], BF16)[:]
        C.rowBb = sb("rowBb", [1, 512], BF16)[:]
        C.rowCb = sb("rowCb", [1, 512], BF16)[:]
        C.rowDb = sb("rowDb", [1, 512], BF16)[:]
        C.onesc_bf_sb = sb("onesc_sb", [128, 1], BF16)
        C.onesr_bf_sb = sb("onesr_sb", [1, 128], BF16)
        C.onesr_f32_sb = sb("onesr32_sb", [1, 128], F32)
        C.ident_sb = sb("ident_sb", [128, 128], BF16)
        C.m01_sb = sb("m01_sb", [128, 256], BF16)
        C.bcols_sb = sb("bcols_sb", [128, 4], F32)
        C.gq_sb = sb("gq_sb", [128, 1], F32)
        C.gk_sb = sb("gk_sb", [128, 1], F32)
        C.psw = [psal(f"psumw{i}", [128, 1024])[:] for i in range(3)]
        C.ps = [C.psw[i // 2][:, 512 * (i % 2): 512 * (i % 2) + 512] for i in range(6)]
        C.pd0 = psal("psumd0", [1, 512])[:]
        C.pd1 = psal("psumd1", [1, 512])[:]
        C.onesc = C.onesc_bf_sb[:]
        C.onesr = C.onesr_bf_sb[:]
        C.onesr32 = C.onesr_f32_sb[:]

        P = Prog(nc)
        C.P = P

        xin3 = _chunk3(C.xin)
        DMA(P, "sp", "xload", C.xf.rearrange("p (c n) -> p c n", c=NCH), xin3, r=[],
            w=[xft(c, h) for c in range(NCH) for h in range(2)])
        for g in range(4):
            cs_ = range(4 * g, 4 * g + 4)
            DMA(P, "pool", f"xbload{g}",
                C.xb[:, 4096 * g: 4096 * g + 4096].rearrange("p (c n) -> p c n", c=4),
                xin3[:, 4 * g: 4 * g + 4, :], r=[],
                w=[xbt(c, h) for c in cs_ for h in range(2)])
        DMA(P, "sp", "c_onesc", C.onesc_bf_sb[:], C.onesc_bf, r=[], w=["onesc"])
        DMA(P, "sp", "c_onesr", C.onesr_bf_sb[:], C.onesr_bf, r=[], w=["onesr"])
        DMA(P, "sp", "c_onesr32", C.onesr_f32_sb[:], C.onesr_f32, r=[], w=["onesr32"])
        DMA(P, "sp", "c_ident", C.ident_sb[:], C.ident, r=[], w=["ident"])
        DMA(P, "sp", "c_m01", C.m01_sb[:], C.m01, r=[], w=["m01"])
        DMA(P, "sp", "c_bcols", C.bcols_sb[:], C.bcols, r=[], w=["bcols"])
        C.b_ln_eps = C.bcols_sb[0:1, 0:1]
        C.b_rms_eps = C.bcols_sb[0:1, 1:2]
        C.b_negshift = C.bcols_sb[:, 2:3]
        if "attn0" in phases:
            DMA(P, "sp", "c_gq", C.gq_sb[:], C.gq, r=[], w=["gq"])
            DMA(P, "sp", "c_gk", C.gk_sb[:], C.gk, r=[], w=["gk"])

        C.prev_tags = []
        for ph in phases:
            PHASE_EMIT[ph](P, C)
            P.marks.append((ph, P.n_array))

        DMA(P, "sp", "xstore", _chunk3(C.xout), C.xf.rearrange("p (c n) -> p c n", c=NCH),
            r=[xft(c, h) for c in range(NCH) for h in range(2)], w=["xoutdone"])
        P.op("sp", lambda e: e.nop(), reads=["xoutdone"], writes=[])

        block = ctx.enter_context(nc.Block())
        P.finalize(block, ctx)
        LAST_MARKS[:] = P.marks
        globals()["LAST_PROG"] = P
    return nc, names



SEGMENTS = [["attn0", "ffn0", "attn1", "moe1"]]


def kernel(**inputs):
    shared, percore = _host_prep(inputs)
    stream = [percore[c]["xin"] for c in range(NCORES)]
    for phases in SEGMENTS:
        nc, names = build_program(phases)
        in_maps = []
        for c in range(NCORES):
            m = {}
            for n in names:
                if n == "xin":
                    m[n] = stream[c]
                elif n in shared:
                    m[n] = shared[n]
                else:
                    m[n] = percore[c][n]
            in_maps.append(m)
        res = run_bass_kernel_spmd(nc, in_maps, core_ids=list(range(NCORES)))
        stream = [res.results[c]["xout"] for c in range(NCORES)]
    return _assemble_output(res.results)


PHASE_EMIT["ffn0"] = emit_ffn0
PHASE_EMIT["attn0"] = emit_attn0
PHASE_EMIT["attn1"] = emit_attn1
PHASE_EMIT["moe1"] = emit_moe
